# version 20
# baseline (speedup 1.0000x reference)
import numpy as np
import ml_dtypes
import concourse.bass as bass
import concourse.mybir as mybir
from concourse.bass_utils import run_bass_kernel_spmd

F32 = mybir.dt.float32
BF16 = mybir.dt.bfloat16
F16 = mybir.dt.float16
I32 = mybir.dt.int32
I16 = mybir.dt.int16
ALU = mybir.AluOpType
AF = mybir.ActivationFunctionType

D = 2048
NCH = 16
T = 1024
NCORES = 8
I_FFN = 5632
GU_COLS = 11264
NSLICE = 22
SCALE = float(128) ** -0.5
EPS_LN = 1e-5
EPS_RMS = 1e-6
KB = 1024

KB_BLOCKS = [(0, 8), (1, 8), (2, 8), (3, 8), (4, 8), (5, 4)]


def _perm_stream():
    p = []
    for A in range(2):
        for j in range(8):
            for cw in range(128):
                p.append(8 * (128 * A + cw) + j)
    return np.array(p, np.int64)


def _perm_ffn():
    p = []
    for cb in range(5):
        for j in range(8):
            for cw in range(128):
                p.append(8 * (128 * cb + cw) + j)
    for a in range(4):
        for c in range(640, 704):
            p.append(8 * c + a)
        for c in range(640, 704):
            p.append(8 * c + a + 4)
    return np.array(p, np.int64)


class Sched:
    ENGINES = ("sync", "gpsimd", "tensor", "scalar", "vector")

    def __init__(self, nc):
        self.nc = nc
        self.ops = {e: [] for e in self.ENGINES}
        self.sems = {}
        self.cnt = {}
        for s in ("SX", "SXL", "SB", "SE", "SW0", "SW1", "SC",
                  "SK0", "SK1", "SG0", "SG1", "SN0", "SN1", "SD0", "SD1",
                  "PE", "AC", "DV", "CC", "GP"):
            self.sems[s] = nc.alloc_semaphore(s)
            self.cnt[s] = 0

    def add(self, eng, fn, waits=(), inc=None, inc_n=1):
        mark = None
        if inc is not None:
            self.cnt[inc] += inc_n
            mark = self.cnt[inc]
        self.ops[eng].append((fn, tuple(waits), inc, inc_n))
        return mark

    def dma(self, eng, sem, out, in_, waits=()):
        return self.add(eng, lambda e, o=out, i=in_: e.dma_start(out=o, in_=i),
                        waits, sem, 16)

    def emit(self):
        with self.nc.Block() as block:
            for eng in self.ENGINES:
                ops = self.ops[eng]

                def body(e, ops=ops):
                    hi = {}
                    for fn, waits, inc, inc_n in ops:
                        for (s, v) in waits:
                            if callable(v):
                                v = v()
                            if hi.get(s, -1) < v:
                                e.wait_ge(self.sems[s], v)
                                hi[s] = v
                        inst = fn(e)
                        if inc is not None:
                            inst.then_inc(self.sems[inc], inc_n)
                getattr(block, eng)(body)


def build_program(debug_tap=None, upto=None):
    nc = bass.Bass("TRN2", target_bir_lowering=False, detect_race_conditions=False)

    x_in = nc.dram_tensor("x_in", [NCH, 128, T], F32, kind="ExternalInput").ap()
    w_q = nc.dram_tensor("w_q", [NCH, 128, D], F32, kind="ExternalInput").ap()
    w_k = nc.dram_tensor("w_k", [NCH, 128, D], F32, kind="ExternalInput").ap()
    w_v = nc.dram_tensor("w_v", [NCH, 128, D], F32, kind="ExternalInput").ap()
    w_o = nc.dram_tensor("w_o", [NCH, 128, D], F32, kind="ExternalInput").ap()
    bias_pack = nc.dram_tensor("bias_pack", [128, 64], F32, kind="ExternalInput").ap()
    pkgu = nc.dram_tensor("pkgu", [NSLICE, 128, 1024], I32, kind="ExternalInput").ap()
    sgu = nc.dram_tensor("sgu", [NSLICE, 128, 2048], F16, kind="ExternalInput").ap()
    pkdn = nc.dram_tensor("pkdn", [768, D], I32, kind="ExternalInput").ap()
    sdn = nc.dram_tensor("sdn", [768, 2 * D], F16, kind="ExternalInput").ap()
    cmask_in = nc.dram_tensor("cmask_in", [8, 1], F32, kind="ExternalInput").ap()
    idin = nc.dram_tensor("idin", [128, 128], F32, kind="ExternalInput").ap()
    out_ext = nc.dram_tensor("out_ext", [NCH, 128, T], F32, kind="ExternalOutput").ap()

    x_park = nc.dram_tensor("x_park", [NCH, 128, T], BF16, kind="Internal").ap()
    xp2 = nc.dram_tensor("xp2", [NCH, 128, T], BF16, kind="Internal").ap()
    cc_in = nc.dram_tensor("cc_in", [1, D], F32, kind="Internal").ap()
    cc_out = nc.dram_tensor("cc_out", [8, D], F32, kind="Internal").ap()

    BASE = (nc.sbuf_base + 31) // 32 * 32
    A0, E0, B0, C0, D0, M0 = (BASE, BASE + 32 * KB, BASE + 64 * KB,
                              BASE + 96 * KB, BASE + 160 * KB, BASE + 192 * KB)
    t_at = nc.alloc_sbuf_tensor_at

    xA = t_at("xA", [128, 16 * T], BF16, offset=A0)
    accF = t_at("accF", [128, 16 * T], F32, offset=A0)
    phiq = t_at("phiq", [128, 16 * T], BF16, offset=E0)
    xE = t_at("xE", [128, 16 * T], BF16, offset=E0)
    xnB = t_at("xnB", [128, 16 * T], BF16, offset=B0)
    wbuf0 = t_at("wbuf0", [128, 16 * T], BF16, offset=C0)
    wbuf1 = t_at("wbuf1", [128, 16 * T], BF16, offset=C0 + 32 * KB)
    vD = t_at("vD", [128, 16 * T], BF16, offset=D0)

    e_scr = t_at("e_scr", [128, 512], F32, offset=M0)
    r_scr = t_at("r_scr", [128, 512], F32, offset=M0 + 2 * KB)
    phi_scr = t_at("phi_scr", [128, 512], BF16, offset=M0 + 4 * KB)
    kv_scr = t_at("kv_scr", [128, T], BF16, offset=M0 + 5 * KB)
    fscr = t_at("fscr", [128, T], F32, offset=M0 + 7 * KB)
    identS = t_at("identS", [128, 128], F32, offset=M0 + 11 * KB)
    biasS = t_at("biasS", [128, 64], F32, offset=M0 + 11 * KB + 512)
    TtS = t_at("TtS", [128, 16], F32, offset=M0 + 11 * KB + 768)
    offsS = t_at("offsS", [128, 16], F32, offset=M0 + 11 * KB + 832)
    cmaskS = t_at("cmaskS", [8, 1], F32, offset=M0 + 11 * KB + 896)
    onesS = t_at("onesS", [128, 1], BF16, offset=M0 + 11 * KB + 928)
    onesRow = t_at("onesRow", [1, 128], F32, offset=M0 + 11 * KB + 960)
    ttrS = t_at("ttrS", [16, 128], F32, offset=M0 + 12 * KB + 448)
    constS = t_at("constS", [1, 8], F32, offset=M0 + 13 * KB)
    conC = t_at("conC", [128, 8], F32, offset=M0 + 13 * KB + 64)

    ln1_M = t_at("ln1_M", [128, T], BF16, offset=D0)
    ln1_R = t_at("ln1_R", [128, T], BF16, offset=D0 + 4 * KB)
    ln1_st = t_at("ln1_st", [1, 2048], F32, offset=D0 + 8 * KB)
    ln1_sq = t_at("ln1_sq", [128, 2 * T], BF16, offset=D0 + 16 * KB)

    g_sb = t_at("g_sb", [8, D], F32, offset=A0)
    mpW = t_at("mpW", [128, 2048], I16, offset=A0 + 8 * KB)
    rrW = t_at("rrW", [128, 2048], BF16, offset=A0 + 12 * KB)
    pkgW = t_at("pkgW", [128, 1024], I32, offset=A0 + 16 * KB)
    sxgW = t_at("sxgW", [128, 2048], F16, offset=A0 + 20 * KB)
    rrW2 = t_at("rrW2", [128, 2048], BF16, offset=A0 + 28 * KB)
    ln2_M = t_at("ln2_M", [128, T], BF16, offset=A0 + 8 * KB)
    ln2_R = t_at("ln2_R", [128, T], BF16, offset=A0 + 12 * KB)
    ln2_st = t_at("ln2_st", [1, 2048], F32, offset=A0 + 16 * KB)
    ln2_sq = t_at("ln2_sq", [128, 2 * T], BF16, offset=A0 + 24 * KB)
    e_scr2 = t_at("e_scr2", [128, 512], F32, offset=A0 + 28 * KB)
    r_scr2 = t_at("r_scr2", [128, 512], F32, offset=A0 + 30 * KB)
    phi_scr2 = t_at("phi_scr2", [128, 512], BF16, offset=A0 + 14 * KB)

    ln3_sq = t_at("ln3_sq", [128, 2 * T], BF16, offset=A0)
    ln3_M = t_at("ln3_M", [128, T], BF16, offset=A0 + 4 * KB)
    ln3_R = t_at("ln3_R", [128, T], BF16, offset=A0 + 6 * KB)
    ln3_st = t_at("ln3_st", [1, 2048], F32, offset=A0 + 8 * KB)

    pkg = t_at("pkg", [128, 2048], I32, offset=C0)
    sxg = t_at("sxg", [128, 4096], F16, offset=C0 + 8 * KB)
    lhsgu = t_at("lhsgu", [128, 16384], BF16, offset=C0 + 16 * KB)
    MPT = t_at("MPT", [128, 2048], I16, offset=C0 + 48 * KB)
    RRT = t_at("RRT", [128, 2048], BF16, offset=C0 + 52 * KB)
    pkdb = t_at("pkdb", [128, 1024], I32, offset=C0 + 56 * KB)
    sxdb = t_at("sxdb", [128, 2048], F16, offset=C0 + 60 * KB)
    A2T = t_at("A2T", [128, 1024], I32, offset=M0 + 7 * KB)
    SMT = t_at("SMT", [128, 2048], I16, offset=M0)
    lhsdn = t_at("lhsdn", [128, 16384], BF16, offset=E0)
    lhstl = t_at("lhstl", [128, 8192], BF16, offset=E0)
    accB = t_at("accB", [128, 16 * T], BF16, offset=A0)
    outB = t_at("outB", [128, 8 * T], F32, offset=B0)
    ln4_st = t_at("ln4_st", [1, 3072], F32, offset=C0 + 16 * KB)
    ln4_M = t_at("ln4_M", [128, T], BF16, offset=C0)
    ln4_R = t_at("ln4_R", [128, T], BF16, offset=C0 + 2 * KB)
    ln4_sq = t_at("ln4_sq", [128, 2 * T], BF16, offset=C0 + 4 * KB)
    ln4_u = None
    fsb = t_at("fsb", [128, T], BF16, offset=M0 + 5 * KB)

    psA = nc.alloc_psum_tensor("psA", [128, 2048], F32)
    psB = nc.alloc_psum_tensor("psB", [128, 2048], F32)

    S = Sched(nc)

    def _finish(buf, waits=()):
        mk = S.dma("gpsimd", "SC", out_ext.rearrange("c p t -> p c t"),
                   buf[:].rearrange("p (c t) -> p c t", c=NCH), list(waits))
        S.add("gpsimd", lambda e: e.nop(), [("SC", mk)])
        S.emit()
        return nc

    def chunk(t, c, lo=0, hi=T):
        return t[:, c * T + lo: c * T + hi]

    S.add("vector", lambda e: e.memset(onesS[:], 1.0))
    S.add("vector", lambda e: e.memset(onesRow[:], 1.0))
    S.add("vector", lambda e: e.memset(constS[0:1, 0:1], EPS_LN))
    S.add("vector", lambda e: e.memset(constS[0:1, 1:2], -0.5))
    S.add("vector", lambda e: e.memset(constS[0:1, 2:3], 1.0 / D))
    S.add("vector", lambda e: e.memset(conC[:, 0:1], -1.0))
    S.add("vector", lambda e: e.memset(conC[:, 1:2], 16128.0), inc="DV")
    mk_ones = S.cnt["DV"]
    mk_bias = S.dma("sync", "SB", biasS[:], bias_pack)
    mk_id = S.dma("sync", "SB", identS[:], idin)
    mk_cm = S.dma("sync", "SB", cmaskS[:], cmask_in)
    mk_x = S.dma("gpsimd", "SXL", xA[:].rearrange("p (c t) -> p c t", c=NCH),
                 x_in.rearrange("c p t -> p c t"))
    wv0 = S.dma("gpsimd", "SW0", wbuf0[:].rearrange("p (c e) -> p c e", c=NCH),
                w_v[:, :, 0:1024].rearrange("c p e -> p c e"))
    wv1 = S.dma("gpsimd", "SW1", wbuf1[:].rearrange("p (c e) -> p c e", c=NCH),
                w_v[:, :, 1024:2048].rearrange("c p e -> p c e"))

    def ln_stats(src_chunk, sq_t, st, M_t, R_t, mk_src, kind="ln", cp_t=None):
        pe_marks = []
        for c in range(NCH):
            half = (c % 2)
            sqv = sq_t[:, half * T:(half + 1) * T]
            waits = [mk_src[c]] if isinstance(mk_src, list) else [mk_src]
            if c >= 2:
                waits = waits + [("PE", pe_marks[c - 2])]
            if kind == "rms_ln" and cp_t is not None:
                cpv = cp_t[:, half * T:(half + 1) * T]
                S.add("scalar", lambda e, s=src_chunk(c), d=cpv: e.activation(
                    out=d, in_=s, func=AF.Identity), waits)
            mkq = S.add("scalar", lambda e, s=src_chunk(c), d=sqv: e.activation(
                out=d, in_=s, func=AF.Square), waits, "AC")
            rhs_sum = (src_chunk(c) if (kind != "rms_ln" or cp_t is None)
                       else cp_t[:, half * T:(half + 1) * T])
            first, last = (c == 0), (c == NCH - 1)
            for (bank, rh) in ((0, rhs_sum), (1, sqv)):
                for n in range(2):
                    mk = S.add("tensor", lambda e, b=bank, n=n, r=rh, f=first, l=last: e.matmul(
                        psA[0:1, (2 * b + n) * 512:(2 * b + n + 1) * 512],
                        onesS[:], r[:, n * 512:(n + 1) * 512],
                        start=f, stop=l),
                        [("AC", mkq), (("DV", mk_ones))], "PE")
            pe_marks.append(mk)
        mk_st = mk
        S.add("scalar", lambda e: e.activation(out=st[0:1, 0:512], in_=psA[0:1, 0:512],
                                               func=AF.Identity, scale=constS[0:1, 2:3]),
              [("PE", mk_st), ("DV", mk_ones)])
        S.add("scalar", lambda e: e.activation(out=st[0:1, 512:1024], in_=psA[0:1, 512:1024],
                                               func=AF.Identity, scale=constS[0:1, 2:3]))
        S.add("scalar", lambda e: e.activation(out=st[0:1, 1024:1536], in_=psA[0:1, 1024:1536],
                                               func=AF.Identity, scale=constS[0:1, 2:3]))
        mk_mu = S.add("scalar", lambda e: e.activation(out=st[0:1, 1536:2048], in_=psA[0:1, 1536:2048],
                                                       func=AF.Identity, scale=constS[0:1, 2:3]), inc="AC")
        mu = st[0:1, 0:1024]
        m2 = st[0:1, 1024:2048]
        for n in range(2):
            mkbM = S.add("tensor", lambda e, n=n: e.matmul(
                psB[:, n * 512:(n + 1) * 512], onesRow[:], mu[:, n * 512:(n + 1) * 512],
                start=True, stop=True), [("AC", mk_mu)], "PE")
        mkM = S.add("vector", lambda e: e.tensor_copy(out=M_t[:], in_=psB[:, 0:1024]),
                    [("PE", mkbM)], "DV")
        mk1 = S.add("vector", lambda e: e.tensor_tensor(
            out=psA[0:1, 0:1024], in0=mu, in1=mu, op=ALU.mult), [("DV", mkM)], "DV")
        if kind == "ln":
            mkv = S.add("vector", lambda e: e.tensor_tensor(
                out=st[0:1, 0:1024], in0=m2, in1=psA[0:1, 0:1024], op=ALU.subtract),
                [("DV", mk1)], "DV")
            eps = EPS_LN
        else:
            mkd = S.add("vector", lambda e: e.tensor_scalar(
                out=st[0:1, 2048:3072], in0=m2, scalar1=1.0 + EPS_LN,
                scalar2=EPS_LN * EPS_RMS, op0=ALU.mult, op1=ALU.add), [("DV", mk1)], "DV")
            mkv = S.add("vector", lambda e: e.tensor_tensor(
                out=st[0:1, 0:1024], in0=st[0:1, 2048:3072], in1=psA[0:1, 0:1024],
                op=ALU.subtract), [("DV", mkd)], "DV")
            eps = 0.0
        bias_ap = constS[0:1, 0:1] if eps else 0.0
        mkl = S.add("scalar", lambda e, b=bias_ap: e.activation(
            out=st[0:1, 1024:2048], in_=st[0:1, 0:1024], func=AF.Ln, bias=b),
            [("DV", mkv)], "AC")
        mke = S.add("scalar", lambda e: e.activation(
            out=st[0:1, 0:1024], in_=st[0:1, 1024:2048], func=AF.Exp,
            scale=constS[0:1, 1:2]),
            [("AC", mkl)], "AC")
        for n in range(2):
            mkbR = S.add("tensor", lambda e, n=n: e.matmul(
                psB[:, (2 + n) * 512:(3 + n) * 512], onesRow[:],
                st[0:1, n * 512:(n + 1) * 512], start=True, stop=True),
                [("AC", mke)], "PE")
        mkR = S.add("vector", lambda e: e.tensor_copy(out=R_t[:], in_=psB[:, 1024:2048]),
                    [("PE", mkbR)], "DV")
        return mkM, mkR

    mk_xc = mk_x
    mkM1, mkR1 = ln_stats(lambda c: chunk(xA, c), ln1_sq, ln1_st, ln1_M, ln1_R,
                          [("SXL", mk_x)] * NCH)
    mk_xn1 = []
    for c in range(NCH):
        S.add("vector", lambda e, c=c: e.tensor_tensor(
            out=fscr[:].bitcast(BF16)[:, 0:T], in0=chunk(xA, c), in1=ln1_M[:], op=ALU.subtract),
            [("DV", mkR1)])
        mk = S.add("vector", lambda e, c=c: e.tensor_tensor(
            out=chunk(xnB, c), in0=fscr[:].bitcast(BF16)[:, 0:T], in1=ln1_R[:], op=ALU.mult), inc="DV")
        mk_xn1.append(("DV", mk))
    mk_xn1_all = mk_xn1[-1]

    if upto == 'xn1':
        return _finish(xnB, [mk_xn1_all])
    mk_park1 = S.dma("sync", "SX", x_park.rearrange("c p t -> p c t"),
                     xA[:].rearrange("p (c t) -> p c t", c=NCH), [mk_xn1_all])

    bank_users = {0: [], 1: []}

    def proj(wname, wmarks, evac, next_load=None, after_group=None, rhs_marks=None):
        g = 0
        for mh in range(2):
            wb = wbuf0 if mh == 0 else wbuf1
            last_mm = None
            for mg in range(2):
                for n in range(2):
                    bank = g % 2
                    ps = psA if bank == 0 else psB
                    waits = [(("SW%d" % mh), wmarks[mh])] + bank_users[bank]
                    if rhs_marks is None:
                        waits = waits + [mk_xn1_all]
                    for mi in range(4):
                        m_glob = mh * 8 + mg * 4 + mi
                        for k in range(NCH):
                            w_ = waits if (mi == 0 and k == 0) else ()
                            if rhs_marks is not None and g == 0 and mi == 0:
                                w_ = list(w_) + [rhs_marks[k]]
                            last_mm = S.add("tensor", lambda e, wb=wb, k=k, mg=mg, mi=mi, n=n, ps=ps: e.matmul(
                                ps[:, mi * 512:(mi + 1) * 512],
                                wb[:, k * 1024 + (mg * 4 + mi) * 128: k * 1024 + (mg * 4 + mi + 1) * 128],
                                chunk(xnB, k, n * 512, (n + 1) * 512),
                                start=(k == 0), stop=(k == NCH - 1)),
                                w_,
                                "PE" if (mi == 3 and k == NCH - 1) else None)
                    mm_mark = ("PE", last_mm)
                    rel = []
                    for mi in range(4):
                        m_glob = mh * 8 + mg * 4 + mi
                        rel += evac(g, m_glob, ps, ps[:, mi * 512:(mi + 1) * 512], n, mm_mark)
                    bank_users[bank] = rel
                    if after_group is not None:
                        after_group(g)
                    g += 1
            if next_load is not None:
                next_load(mh, ("PE", last_mm))

    wk_marks = [None, None]

    def load_wk(mh, after):
        src = w_k[:, :, mh * 1024:(mh + 1) * 1024].rearrange("c p e -> p c e")
        dst = (wbuf0 if mh == 0 else wbuf1)[:].rearrange("p (c e) -> p c e", c=NCH)
        wk_marks[mh] = S.dma("gpsimd", "SW%d" % mh, dst, src, [after])

    def evac_v(g, m, ps, tile, n, mm_mark):
        mk = S.add("scalar", lambda e, m=m, tile=tile, n=n: e.activation(
            out=chunk(vD, m, n * 512, (n + 1) * 512), in_=tile,
            func=AF.Identity, bias=biasS[:, 32 + m:33 + m]),
            [mm_mark, ("SB", 48)], "AC")
        return [("AC", mk)]

    proj("wv", (wv0, wv1), evac_v, load_wk, rhs_marks=mk_xn1)
    mk_v = S.cnt["AC"]
    if upto == 'v':
        return _finish(vD, [('AC', mk_v)])

    wq_marks = [None, None]

    def load_wq(mh, after):
        src = w_q[:, :, mh * 1024:(mh + 1) * 1024].rearrange("c p e -> p c e")
        dst = (wbuf0 if mh == 0 else wbuf1)[:].rearrange("p (c e) -> p c e", c=NCH)
        wq_marks[mh] = S.dma("gpsimd", "SW%d" % mh, dst, src, [after])

    scan_done = [None] * NCH
    ev_state = {"dv_last": None, "ctr": 0, "dv_par": [None, None]}

    def evac_k(g, m, ps, tile, n, mm_mark):
        par = ev_state["ctr"] % 2
        ev_state["ctr"] += 1
        es_, rs_, phs_ = ((e_scr, r_scr, phi_scr) if par == 0
                          else (e_scr2, r_scr2, phi_scr2))
        w0 = [mm_mark, ("SB", 48)]
        if ev_state["dv_par"][par] is not None:
            w0 = w0 + [("DV", ev_state["dv_par"][par])]
        S.add("scalar", lambda e, tile=tile, es_=es_: e.activation(
            out=es_[:], in_=tile, func=AF.Exp, bias=biasS[:, 16 + m:17 + m]), w0)
        mke = S.add("scalar", lambda e, tile=tile, rs_=rs_: e.activation(
            out=rs_[:], in_=tile, func=AF.Relu, bias=biasS[:, 16 + m:17 + m]), inc="AC")
        S.add("vector", lambda e, es_=es_, rs_=rs_, phs_=phs_: e.scalar_tensor_tensor(
            out=phs_[:], in0=es_[:], scalar=1.0, in1=rs_[:],
            op0=ALU.min, op1=ALU.add), [("AC", mke)])
        mkv2 = S.add("vector", lambda e, m=m, n=n, phs_=phs_: e.scalar_tensor_tensor(
            out=chunk(vD, m, n * 512, (n + 1) * 512), in0=phs_[:], scalar=SCALE,
            in1=chunk(vD, m, n * 512, (n + 1) * 512),
            op0=ALU.mult, op1=ALU.mult), inc="DV")
        ev_state["dv_par"][par] = mkv2
        ev_state["last_ac"] = mke
        if n == 1:
            mksc = S.add("vector", lambda e, m=m: e.tensor_tensor_scan(
                out=fscr[:], data0=chunk(vD, m), data1=chunk(vD, m), initial=0.0,
                op0=ALU.add, op1=ALU.bypass), (), "DV")
            S.add("vector", lambda e, m=m: e.tensor_copy(
                out=chunk(vD, m), in_=fscr[:]), [("DV", mksc)])
            mk3 = S.add("vector", lambda e, m=m: e.tensor_copy(
                out=TtS[:, m:m + 1], in_=fscr[:, T - 1:T]), (), "DV")
            scan_done[m] = mk3
            ccs = "SB" if m % 2 == 0 else "SE"
            cc_dma[m % 2] = (ccs, S.dma("sync", ccs,
                             cc_in.rearrange("a (c p) -> (a p) c", c=16)[:, m:m + 1],
                             TtS[:, m:m + 1], [("DV", mk3)]))
        return [("AC", ev_state["last_ac"])]

    cc_dma = [None, None]
    proj("wk", wk_marks, evac_k, load_wq)
    mk_scans_pre = ("DV", scan_done[NCH - 1])
    mk_coll = S.add("gpsimd", lambda e: e.collective_compute(
        "AllGather", ALU.bypass, replica_groups=[list(range(NCORES))],
        ins=[cc_in.opt()], outs=[cc_out.opt()]),
        [(cc_dma[0][0], cc_dma[0][1]), (cc_dma[1][0], cc_dma[1][1])], "CC")
    mk_gsb = S.dma("gpsimd", "SC", g_sb[:], cc_out,
                   [("CC", mk_coll), ("SX", mk_park1)])

    if upto == 'kvs':
        return _finish(vD, [mk_scans_pre])
    wo_marks = [None, None]

    def load_wo(mh, after):
        src = w_o[:, :, mh * 1024:(mh + 1) * 1024].rearrange("c p e -> p c e")
        dst = (wbuf0 if mh == 0 else wbuf1)[:].rearrange("p (c e) -> p c e", c=NCH)
        wo_marks[mh] = S.dma("gpsimd", "SW%d" % mh, dst, src, [after])

    def evac_q(g, m, ps, tile, n, mm_mark):
        par = ev_state["ctr"] % 2
        ev_state["ctr"] += 1
        es_, rs_ = (e_scr, r_scr) if par == 0 else (e_scr2, r_scr2)
        w0 = [mm_mark, ("SB", 48)]
        if ev_state["dv_par"][par] is not None:
            w0 = w0 + [("DV", ev_state["dv_par"][par])]
        S.add("scalar", lambda e, tile=tile, es_=es_: e.activation(
            out=es_[:], in_=tile, func=AF.Exp, bias=biasS[:, m:m + 1]), w0)
        mke = S.add("scalar", lambda e, tile=tile, rs_=rs_: e.activation(
            out=rs_[:], in_=tile, func=AF.Relu, bias=biasS[:, m:m + 1]), inc="AC")
        mkp = S.add("vector", lambda e, m=m, n=n, es_=es_, rs_=rs_: e.scalar_tensor_tensor(
            out=chunk(phiq, m, n * 512, (n + 1) * 512), in0=es_[:], scalar=1.0,
            in1=rs_[:], op0=ALU.min, op1=ALU.add), [("AC", mke)], "DV")
        ev_state["dv_par"][par] = mkp
        if n == 1 and m >= 12:
            mkc = S.add("vector", lambda e, m=m: e.scalar_tensor_tensor(
                out=chunk(vD, m), in0=chunk(vD, m), scalar=offsS[:, m:m + 1],
                in1=chunk(phiq, m), op0=ALU.add, op1=ALU.mult),
                [("DV", offs_mark[0])], "DV")
            ctx_marks.append(("DV", mkc))
        return [("AC", mke)]

    ev_state["dv_par"] = [None, None]
    ctx_marks = []
    offs_mark = [None]

    def defer_ctx(ms):
        for m in ms:
            mkc = S.add("vector", lambda e, m=m: e.scalar_tensor_tensor(
                out=chunk(vD, m), in0=chunk(vD, m), scalar=offsS[:, m:m + 1],
                in1=chunk(phiq, m), op0=ALU.add, op1=ALU.mult),
                [("DV", offs_mark[0])], "DV")
            ctx_marks.append(("DV", mkc))

    def after_wq_group(g):
        if g == 6:
            for c in range(NCH):
                mk_omm = S.add("tensor", lambda e, c=c: e.matmul(
                    psB[:, c:c + 1], g_sb[:, c * 128:(c + 1) * 128], cmaskS[:],
                    start=True, stop=True),
                    ([("SC", mk_gsb), ("SB", 48)] + bank_users[1]) if c == 0 else (),
                    "PE" if c == NCH - 1 else None)
            mko = S.add("vector", lambda e: e.tensor_copy(
                out=offsS[:], in_=psB[:, 0:16]), [("PE", mk_omm)], "DV")
            offs_mark[0] = mko
            bank_users[1] = bank_users[1] + [("DV", mko)]
        elif g == 7:
            defer_ctx(range(0, 12))

    proj("wq", wq_marks, evac_q, load_wo, after_group=after_wq_group)
    mk_phiq = ("DV", S.cnt["DV"])

    if upto == 'phiq':
        return _finish(phiq, [mk_phiq])
    mk_ctx_all = ctx_marks[-1]

    mk_rel1 = S.dma("sync", "SX", xE[:].rearrange("p (c t) -> p c t", c=NCH),
                    x_park.rearrange("c p t -> p c t"), [mk_ctx_all])

    scr_a2_free = [None, None]
    scr_mp_free = [None, None]
    scr_rr_free = [None, None]
    a2_marks = [None, None]
    act_marks = [None, None]

    def dq_extract(par, jp, wslab, extra_waits=()):
        a2 = A2T[:, par * 512:(par + 1) * 512]
        sh = 6 - 4 * jp
        op = ALU.logical_shift_left if sh >= 0 else ALU.logical_shift_right
        w0 = list(extra_waits)
        if scr_a2_free[par] is not None:
            w0.append(scr_a2_free[par])
        mka2 = S.add("vector", lambda e, a2=a2, w=wslab, sh=abs(sh), op=op: e.tensor_scalar(
            out=a2, in0=w, scalar1=sh, scalar2=0x01C001C0,
            op0=op, op1=ALU.bitwise_and), w0, "DV")
        shs = 12 - 4 * jp
        S.add("vector", lambda e, w=wslab, shs=shs, par=par: e.tensor_scalar(
            out=SMT[:, par * 1024:(par + 1) * 1024].bitcast(I32), in0=w, scalar1=shs,
            scalar2=0x80008000 - (1 << 32),
            op0=ALU.logical_shift_left, op1=ALU.bitwise_and))
        a2_marks[par] = mka2

    def dq_act(par):
        a2 = A2T[:, par * 512:(par + 1) * 512]
        mp = MPT[:, par * 1024:(par + 1) * 1024]
        rr = RRT[:, par * 1024:(par + 1) * 1024]
        wmp = [("DV", a2_marks[par])]
        if scr_mp_free[par] is not None:
            wmp.append(scr_mp_free[par])
        mkmp = S.add("scalar", lambda e, a2=a2, mp=mp: e.activation(
            out=mp, in_=a2.bitcast(I16), func=AF.Identity, bias=conC[:, 1:2]),
            wmp, "AC")
        wrr = [("AC", mkmp)]
        if scr_rr_free[par] is not None:
            wrr.append(scr_rr_free[par])
        mkr = S.add("scalar", lambda e, mp=mp, rr=rr: e.activation(
            out=rr, in_=mp.bitcast(BF16), func=AF.Relu,
            scale=conC[:, 0:1], bias=1.0), wrr, "AC")
        scr_a2_free[par] = ("AC", mkmp)
        act_marks[par] = mkr

    def dq_consume(par, sxslab, out_ap, tail_a=None, tailtile=None):
        mp = MPT[:, par * 1024:(par + 1) * 1024]
        rr = RRT[:, par * 1024:(par + 1) * 1024]
        smt = SMT[:, par * 1024:(par + 1) * 1024]
        mksub = S.add("vector", lambda e, mp=mp, rr=rr: e.tensor_tensor(
            out=rr, in0=mp.bitcast(BF16), in1=rr, op=ALU.subtract),
            [("AC", act_marks[par])], "DV")
        mkor = S.add("vector", lambda e, rr=rr, smt=smt: e.tensor_tensor(
            out=smt, in0=rr.bitcast(I16), in1=smt, op=ALU.bitwise_or), (), "DV")
        scr_mp_free[par] = ("DV", mksub)
        scr_rr_free[par] = ("DV", mkor)
        vs = smt.bitcast(BF16)
        if tail_a is None:
            mkmul = S.add("vector", lambda e, vs=vs, sx=sxslab, o=out_ap: e.tensor_tensor(
                out=o, in0=vs, in1=sx, op=ALU.mult), (), "DV")
        else:
            vs3 = vs.rearrange("p (n two) -> p n two", two=2)
            sx3 = sxslab.rearrange("p (n two) -> p n two", two=2)
            S.add("vector", lambda e, vs3=vs3, sx3=sx3, t=tailtile: e.tensor_tensor(
                out=t[0:64, :], in0=vs3[0:64, :, 0], in1=sx3[0:64, :, 0], op=ALU.mult))
            mkmul = S.add("vector", lambda e, vs3=vs3, sx3=sx3, t=tailtile: e.tensor_tensor(
                out=t[64:128, :], in0=vs3[64:128, :, 1], in1=sx3[64:128, :, 1],
                op=ALU.mult), (), "DV")
        return ("DV", mkmul)

    mpW_free = [None, None]
    vsgnW_free = [None] * 4
    actW_marks = [None, None]
    wv_ctr = [0]

    def dqW_act(par):
        a2 = A2T[:, par * 512:(par + 1) * 512]
        mp = mpW[:, par * 1024:(par + 1) * 1024]
        wmp = [("DV", a2_marks[par])]
        if mpW_free[par] is not None:
            wmp.append(mpW_free[par])
        mkmp = S.add("scalar", lambda e, a2=a2, mp=mp: e.activation(
            out=mp, in_=a2.bitcast(I16), func=AF.Identity, bias=conC[:, 1:2]),
            wmp, "AC")
        scr_a2_free[par] = ("AC", mkmp)
        actW_marks[par] = mkmp

    def dqW_consume(par, sxslab, out_ap, gate):
        mp = mpW[:, par * 1024:(par + 1) * 1024]
        smt = SMT[:, par * 1024:(par + 1) * 1024]
        vs = wv_ctr[0] % 4
        wv_ctr[0] += 1
        slot = (rrW if vs < 2 else rrW2)[:, (vs % 2) * 1024:(vs % 2 + 1) * 1024]
        wt = [("AC", actW_marks[par])]
        if vsgnW_free[vs] is not None:
            wt.append(vsgnW_free[vs])
        S.add("vector", lambda e, mp=mp, t1=slot: e.tensor_scalar(
            out=t1, in0=mp.bitcast(BF16), scalar1=2.0, scalar2=-1.0,
            op0=ALU.mult, op1=ALU.add), wt)
        mkmin = S.add("vector", lambda e, mp=mp, t1=slot: e.tensor_tensor(
            out=t1, in0=t1, in1=mp.bitcast(BF16), op=ALU.min), (), "DV")
        mkor = S.add("vector", lambda e, t1=slot, smt=smt: e.tensor_tensor(
            out=t1.bitcast(I16), in0=t1.bitcast(I16), in1=smt,
            op=ALU.bitwise_or), (), "DV")
        mpW_free[par] = ("DV", mkmin)
        mkmul = S.add("gpsimd", lambda e, t1=slot, sx=sxslab, o=out_ap: e.tensor_tensor(
            out=o, in0=t1.bitcast(BF16), in1=sx, op=ALU.mult),
            [gate, ("DV", mkor)], "GP")
        vsgnW_free[vs] = ("GP", mkmul)
        return ("GP", mkmul)

    def dqW_unit_gen(s, gate):
        sp = s % 2
        specs = []
        for A in range(2):
            for jpp in range(4):
                pp = A * 4 + jpp
                wsl = (pkgW[:, A * 512:(A + 1) * 512] if s == 0 else
                       pkg[:, sp * 1024 + A * 512: sp * 1024 + (A + 1) * 512])
                xsl = (sxgW[:, A * 1024:(A + 1) * 1024] if s == 0 else
                       sxg[:, sp * 2048 + A * 1024: sp * 2048 + A * 1024 + 1024])
                specs.append((jpp, wsl, xsl,
                              lhsgu[:, sp * 8192 + pp * 1024: sp * 8192 + (pp + 1) * 1024]))
        n = len(specs)
        pars = [None] * n
        last = None
        for i in range(n + 2):
            if i >= 2:
                jp, w, sx, out = specs[i - 2]
                last = dqW_consume(pars[i - 2], sx, out, gate)
            if i < n:
                jp, w, sx, out = specs[i]
                par = chain_par[0] % 2
                chain_par[0] += 1
                pars[i] = par
                ew = [("SK%d" % sp, pk_marks[s]), ("SG%d" % sp, sx_marks[s])] if i == 0 else ()
                dq_extract(par, jp, w, ew)
                dqW_act(par)
            yield
        dq_marks[("gu", s)] = last

    def lgu_view(buf, A, j, mi):
        pp = A * 4 + (j % 4)
        lane = j // 4
        t3 = lhsgu[:, buf * 8192 + pp * 1024: buf * 8192 + (pp + 1) * 1024]
        return t3.rearrange("p (n two) -> p n two", two=2)[:, mi * 128:(mi + 1) * 128, lane]

    def ldn_view(j, m):
        pp = j % 4
        lane = j // 4
        t3 = lhsdn[:, pp * 4096:(pp + 1) * 4096]
        return t3.rearrange("p (n two) -> p n two", two=2)[:, m * 128:(m + 1) * 128, lane]

    lhsgu_free = [None, None]
    sil_user = [None, None]
    pk_marks = [None] * NSLICE
    sx_marks = [None] * NSLICE
    acc_init = {}
    hblk_marks = [None, None]
    lhsdn_free = [None]
    chain_par = [0]
    dq_marks = {}
    hs_ready = [None]

    def issue_gu_dma(s, waits):
        pk_marks[s] = S.dma("sync", "SK%d" % (s % 2),
                            pkg[:, (s % 2) * 1024:(s % 2 + 1) * 1024],
                            pkgu[s], waits)
        sx_marks[s] = S.dma("sync", "SG%d" % (s % 2),
                            sxg[:, (s % 2) * 2048:(s % 2 + 1) * 2048],
                            sgu[s], waits)

    def dq_unit_gen(u):
        if u[0] == "gu":
            s = u[1]
            sp = s % 2
            specs = []
            for A in range(2):
                for jpp in range(4):
                    pp = A * 4 + jpp
                    specs.append(dict(
                        jp=jpp,
                        w=pkg[:, sp * 1024 + A * 512: sp * 1024 + (A + 1) * 512],
                        sx=sxg[:, sp * 2048 + A * 1024: sp * 2048 + A * 1024 + 1024],
                        out=lhsgu[:, sp * 8192 + pp * 1024: sp * 8192 + (pp + 1) * 1024],
                        waits=[("SK%d" % sp, pk_marks[s]), ("SG%d" % sp, sx_marks[s])],
                        tail=None))
            specs[0]["waits"] = list(specs[0]["waits"])
            if lhsgu_free[sp] is not None:
                specs[0]["waits"].append(lhsgu_free[sp])
            key = ("gu", s)
            dma_cb = None
        else:
            kbi, cb = u[1], u[2]
            tailmode = (cb == 5)
            specs = []
            dmarks = {}

            def issue_dn_dma(cs, extra):
                cp = cs % 2
                mkp = S.dma("sync", "SN%d" % cp, pkdb[:, cp * 512:(cp + 1) * 512],
                            pkdn[128 * cb:128 * (cb + 1), 512 * cs:512 * (cs + 1)], extra)
                mks = S.dma("sync", "SD%d" % cp, sxdb[:, cp * 1024:(cp + 1) * 1024],
                            sdn[128 * cb:128 * (cb + 1), 1024 * cs:1024 * (cs + 1)], extra)
                dmarks[cs] = [("SN%d" % cp, mkp), ("SD%d" % cp, mks)]
            w0 = [lhsdn_free[0]] if lhsdn_free[0] is not None else []
            issue_dn_dma(0, w0)
            issue_dn_dma(1, w0)
            for cs in range(4):
                cp = cs % 2
                for jp in range(4):
                    specs.append(dict(
                        jp=jp,
                        w=pkdb[:, cp * 512:(cp + 1) * 512],
                        sx=sxdb[:, cp * 1024:(cp + 1) * 1024],
                        out=(None if tailmode else
                             lhsdn[:, jp * 4096 + cs * 1024: jp * 4096 + (cs + 1) * 1024]),
                        waits=None, dmarks=dmarks,
                        tail=(lhstl[:, jp * 2048 + cs * 512: jp * 2048 + (cs + 1) * 512]
                              if tailmode else None),
                        cs=cs))
            key = ("dn", kbi)
            dma_cb = issue_dn_dma

        n = len(specs)
        pars = [None] * n
        last = None
        for i in range(n + 2):
            if i >= 2:
                sp_ = specs[i - 2]
                par = pars[i - 2]
                if sp_["tail"] is None:
                    last = dq_consume(par, sp_["sx"], sp_["out"])
                else:
                    last = dq_consume(par, sp_["sx"], None, tail_a=sp_["jp"],
                                      tailtile=sp_["tail"])
                if dma_cb is not None and (i - 2) % 4 == 3:
                    cs_done = (i - 2) // 4
                    if cs_done + 2 < 4:
                        dma_cb(cs_done + 2, [last])
            if i < n:
                sp_ = specs[i]
                par = chain_par[0] % 2
                chain_par[0] += 1
                pars[i] = par
                ew = (sp_["waits"] if sp_.get("waits") is not None
                      else sp_["dmarks"][sp_["cs"]])
                dq_extract(par, sp_["jp"], sp_["w"], ew)
                dq_act(par)
            yield
        dq_marks[key] = last

    def drain(g):
        for _ in g:
            pass

    def lgu_view(buf, A, j, mi):
        pp = A * 4 + (j % 4)
        lane = j // 4
        t3 = lhsgu[:, buf * 8192 + pp * 1024: buf * 8192 + (pp + 1) * 1024]
        return t3.rearrange("p (n two) -> p n two", two=2)[:, mi * 128:(mi + 1) * 128, lane]

    def ldn_view(j, m):
        pp = j % 4
        lane = j // 4
        t3 = lhsdn[:, pp * 4096:(pp + 1) * 4096]
        return t3.rearrange("p (n two) -> p n two", two=2)[:, m * 128:(m + 1) * 128, lane]

    def gen_mm_gu(s):
        sp = s % 2
        last_mm = None
        for n in range(2):
            ps = psA if n == 0 else psB
            waits = [dq_marks[("gu", s)], mk_xn3_all] + bank_users[n]
            for mi in range(4):
                for A in range(2):
                    for j in range(8):
                        k = A * 8 + j
                        last_mm = S.add("tensor", lambda e, A=A, j=j, mi=mi, n=n, ps=ps, k=k, sp=sp: e.matmul(
                            ps[:, mi * 512:(mi + 1) * 512],
                            lgu_view(sp, A, j, mi),
                            chunk(xnB, k, n * 512, (n + 1) * 512),
                            start=(k == 0), stop=(k == 15)),
                            waits if (mi == 0 and k == 0) else (),
                            "PE" if (mi == 3 and k == 15) else None)
            dq_marks[("gumm", s, n)] = ("PE", last_mm)
        lhsgu_free[sp] = ("PE", last_mm)

    def gen_ev_gu(s, kbi, si):
        buf = kbi % 2
        hbase = buf * 8
        rel_all = []
        for n in range(2):
            ps = psA if n == 0 else psB
            mm_mark = dq_marks[("gumm", s, n)]
            rel = []
            for half in range(2):
                hs_chunk = hbase + 2 * si + half
                silv = kv_scr[:, half * 512:(half + 1) * 512]
                wv_ = [mm_mark]
                if sil_user[half] is not None:
                    wv_ = wv_ + [sil_user[half]]
                if si == 0 and n == 0 and half == 0:
                    if hblk_marks[buf] is not None:
                        wv_ = wv_ + [hblk_marks[buf]]
                    if kbi == 0:
                        wv_ = wv_ + [("SX", mk_park2)]
                mksil = S.add("scalar", lambda e, ps=ps, half=half, silv=silv: e.activation(
                    out=silv, in_=ps[:, half * 512:(half + 1) * 512], func=AF.Silu),
                    wv_, "AC")
                mkh = S.add("vector", lambda e, ps=ps, half=half, silv=silv, hs=hs_chunk, n=n: e.tensor_tensor(
                    out=chunk(vD, hs, n * 512, (n + 1) * 512), in0=silv,
                    in1=ps[:, (2 + half) * 512:(3 + half) * 512], op=ALU.mult),
                    [("AC", mksil)], "DV")
                sil_user[half] = ("DV", mkh)
                rel += [("DV", mkh)]
            bank_users[n] = rel
            rel_all += rel
        return rel_all

    def gen_dn_unit(kbi, cb, nch_k, nxt_gen):
        buf = kbi % 2
        hbase = buf * 8
        flush_marks = [None] * 8
        if upto == 'hsdump' and kbi == 0:
            mkt = S.dma("gpsimd", "SC", out_ext.rearrange("c p t -> p c t"),
                        vD[:].rearrange("p (c t) -> p c t", c=NCH),
                        [dq_marks[("dn", kbi)], hs_ready[0]])
            S.add("gpsimd", lambda e: e.nop(), [("SC", mkt)])
            S.emit()
            return nc, True
        acc_hq_marks = {}
        for h in range(8):
            mq, n = h // 2, h % 2
            ps = psA if h % 2 == 0 else psB
            if h < 2:
                waits = [dq_marks[("dn", kbi)], hs_ready[0]] + bank_users[h % 2]
            else:
                waits = [flush_marks[h - 2]]
            last_mm = None
            for mi in range(4):
                m_glob = mq * 4 + mi
                for kk in range(nch_k):
                    j = kk
                    hs = hbase + kk
                    if cb < 5:
                        def mmfn(e, ps=ps, n=n, j=j, m_glob=m_glob, mi=mi, hs=hs, nk=nch_k):
                            return e.matmul(
                                ps[:, mi * 512:(mi + 1) * 512],
                                ldn_view(j, m_glob),
                                chunk(vD, hs, n * 512, (n + 1) * 512),
                                start=(j == 0), stop=(j == nk - 1))
                    else:
                        def mmfn(e, ps=ps, n=n, j=j, m_glob=m_glob, mi=mi, hs=hs, nk=nch_k):
                            return e.matmul(
                                ps[:, mi * 512:(mi + 1) * 512],
                                lhstl[:, j * 2048 + m_glob * 128: j * 2048 + (m_glob + 1) * 128],
                                chunk(vD, hs, n * 512, (n + 1) * 512),
                                start=(j == 0), stop=(j == nk - 1))
                    last_mm = S.add("tensor", mmfn,
                                    waits if (mi == 0 and kk == 0) else (),
                                    "PE" if (mi == 3 and kk == nch_k - 1) else None)
            mm_mark = ("PE", last_mm)
            if nxt_gen is not None:
                for _ in range(3):
                    try:
                        next(nxt_gen)
                    except StopIteration:
                        nxt_gen = None
                        break
            mkf = None
            for mi in range(4):
                m_glob = mq * 4 + mi
                key = (m_glob, n)
                dst = accB[:, m_glob * T + n * 512: m_glob * T + (n + 1) * 512]
                pt = ps[:, mi * 512:(mi + 1) * 512]
                if key not in acc_init:
                    w_ = [mm_mark, ("SX", mk_park2), mk_ctx_all]
                    mkf = S.add("vector", lambda e, dst=dst, pt=pt: e.tensor_copy(
                        out=dst, in_=pt), w_, "DV")
                    acc_init[key] = True
                else:
                    mkf = S.add("vector", lambda e, dst=dst, pt=pt: e.tensor_tensor(
                        out=dst, in0=dst, in1=pt, op=ALU.add), [mm_mark], "DV")
            flush_marks[h] = ("DV", mkf)
            for mi in range(4):
                acc_hq_marks[(mq * 4 + mi, n)] = ("DV", mkf)
            if upto == 'hq%d' % h and kbi == 0:
                mkt = S.dma("gpsimd", "SC", out_ext[0:4, :, 0:512].rearrange("c p t -> p c t"),
                            accB[:, 0:4 * T].rearrange("p (c t) -> p c t", c=4)[:, :, 0:512],
                            [flush_marks[h]])
                S.add("gpsimd", lambda e: e.nop(), [("SC", mkt)])
                S.emit()
                return nc, True
        bank_users[0] = [flush_marks[6]]
        bank_users[1] = [flush_marks[7]]
        hblk_marks[buf] = flush_marks[7]
        lhsdn_free[0] = ("PE", last_mm)
        acc_done[0] = acc_hq_marks
        return flush_marks[7], nxt_gen

    ao_marks = [None] * NCH
    wo_half_marks = [None, None]

    def evac_o(g, m, ps, tile, n, mm_mark):
        mk = S.add("scalar", lambda e, m=m, tile=tile, n=n: e.activation(
            out=chunk(xnB, m, n * 512, (n + 1) * 512), in_=tile,
            func=AF.Identity, bias=biasS[:, 48 + m:49 + m]), [mm_mark], "AC")
        if n == 1:
            ao_marks[m] = ("AC", mk)
        return [("AC", mk)]

    g = 0
    wo_gens = []

    def wo_step(k):
        while k > 0 and wo_gens:
            try:
                next(wo_gens[0])
                k -= 1
            except StopIteration:
                wo_gens.pop(0)

    pk_marks[0] = S.dma("sync", "SK0", pkgW[:], pkgu[0], [("SX", mk_park1)])
    sx_marks[0] = S.dma("sync", "SG0", sxgW[:], sgu[0], [("SX", mk_park1)])
    wo_gens.append(dqW_unit_gen(0, ("PE", lambda: wo_half_marks[0][1])))
    for mh in range(2):
        wb = wbuf0 if mh == 0 else wbuf1
        if mh == 1:
            issue_gu_dma(1, [wo_half_marks[0]])
            wo_gens.append(dqW_unit_gen(1, ("PE", lambda: wo_half_marks[1][1])))
        for mg in range(2):
            for n in range(2):
                bank = g % 2
                ps = psA if bank == 0 else psB
                waits = [(("SW%d" % mh), wo_marks[mh]), mk_ctx_all, mk_xn1_all] + bank_users[bank]
                for mi in range(4):
                    for k in range(NCH):
                        last_mm = S.add("tensor", lambda e, wb=wb, k=k, mg=mg, mi=mi, n=n, ps=ps: e.matmul(
                            ps[:, mi * 512:(mi + 1) * 512],
                            wb[:, k * 1024 + (mg * 4 + mi) * 128: k * 1024 + (mg * 4 + mi + 1) * 128],
                            chunk(vD, k, n * 512, (n + 1) * 512),
                            start=(k == 0), stop=(k == NCH - 1)),
                            waits if (mi == 0 and k == 0) else (),
                            "PE" if (mi == 3 and k == NCH - 1) else None)
                mm_mark = ("PE", last_mm)
                rel = []
                for mi in range(4):
                    m_glob = mh * 8 + mg * 4 + mi
                    rel += evac_o(g, m_glob, ps, ps[:, mi * 512:(mi + 1) * 512], n, mm_mark)
                bank_users[bank] = rel
                g += 1
                wo_step(5)
        wo_half_marks[mh] = ("PE", last_mm)
    wo_step(99)
    mk_ao = ("AC", S.cnt["AC"])
    pre_gates = {"wo_h0": wo_half_marks[0], "wo_all": wo_half_marks[1]}

    if upto == 'ao':
        return _finish(xnB, [mk_ao])
    units = []
    for kbi, (cb, nch_dn) in enumerate(KB_BLOCKS):
        nsl = 4 if cb < 5 else 2
        for si in range(nsl):
            s = (4 * kbi + si) if cb < 5 else (20 + si)
            units.append(("gu", s, kbi, si))
        units.append(("dn", kbi, cb, nch_dn))

    mkM2, mkR2 = ln_stats(lambda c: chunk(xnB, c), ln2_sq, ln2_st, ln2_M, ln2_R,
                          ao_marks)
    mk_xp = []
    for c in range(NCH):
        S.add("vector", lambda e, c=c: e.tensor_tensor(
            out=fscr[:].bitcast(BF16)[:, 0:T], in0=chunk(xnB, c), in1=ln2_M[:], op=ALU.subtract),
            [("DV", mkR2), mk_ctx_all])
        S.add("vector", lambda e, c=c: e.tensor_tensor(
            out=chunk(xnB, c), in0=fscr[:].bitcast(BF16)[:, 0:T], in1=ln2_R[:], op=ALU.mult))
        mk = S.add("vector", lambda e, c=c: e.tensor_tensor(
            out=chunk(vD, c), in0=chunk(xE, c), in1=chunk(xnB, c), op=ALU.add),
            [("SX", mk_rel1)], "DV")
        mk_xp.append(("DV", mk))
    mk_xp_all = mk_xp[-1]

    if upto == 'xp':
        return _finish(vD, [mk_xp_all])

    mkM3, mkR3 = ln_stats(lambda c: chunk(vD, c), ln3_sq, ln3_st, ln3_M, ln3_R,
                          mk_xp)
    mk_xn3 = []
    for c in range(NCH):
        S.add("vector", lambda e, c=c: e.tensor_tensor(
            out=fscr[:].bitcast(BF16)[:, 0:T], in0=chunk(vD, c), in1=ln3_M[:], op=ALU.subtract),
            [("DV", mkR3)])
        mk = S.add("vector", lambda e, c=c: e.tensor_tensor(
            out=chunk(xnB, c), in0=fscr[:].bitcast(BF16)[:, 0:T], in1=ln3_R[:], op=ALU.mult), inc="DV")
        mk_xn3.append(("DV", mk))
    mk_xn3_all = mk_xn3[-1]

    if upto == 'xn3':
        return _finish(xnB, [mk_xn3_all])
    mk_park2 = S.dma("sync", "SX", xp2.rearrange("c p t -> p c t"),
                     vD[:].rearrange("p (c t) -> p c t", c=NCH), [mk_xp_all, mk_xn3_all])

    mk_down = None
    pend_gen = None
    carry_gen = None
    acc_done = [None]
    for i, u in enumerate(units):
        nxt = units[i + 1] if i + 1 < len(units) else None
        nxt2 = units[i + 2] if i + 2 < len(units) else None
        if u[0] == "gu":
            _, s, kbi, si = u
            if pend_gen is not None:
                drain(pend_gen)
                pend_gen = None
            gen_mm_gu(s)
            if s + 2 < NSLICE:
                issue_gu_dma(s + 2, [dq_marks[("gu", s)]])
            if nxt is not None:
                if nxt[0] == "gu" and nxt[1] < 2:
                    pass
                elif nxt[0] == "dn":
                    if carry_gen is None:
                        carry_gen = dq_unit_gen(nxt[:3])
                    drain(carry_gen)
                    carry_gen = None
                else:
                    drain(dq_unit_gen(nxt))
            hs_last = gen_ev_gu(s, kbi, si)
            if nxt2 is not None and nxt2[0] == "dn":
                carry_gen = dq_unit_gen(nxt2[:3])
                for _ in range(7):
                    try:
                        next(carry_gen)
                    except StopIteration:
                        carry_gen = None
                        break
            if si == (3 if kbi < 5 else 1):
                hs_ready[0] = hs_last[-1]
            if upto == 'hblk%d' % kbi and si == (3 if kbi < 5 else 1):
                return _finish(vD, [hs_last[-1]])
        else:
            _, kbi, cb, nch_dn = u
            g = dq_unit_gen(nxt) if nxt is not None else None
            r = gen_dn_unit(kbi, cb, nch_dn, g)
            if isinstance(r[1], bool):
                return r[0]
            mk_down, g = r
            pend_gen = g
            if upto == 'acc%d' % kbi:
                return _finish(accB, [mk_down])

    if upto == 'y':
        mky = S.dma("gpsimd", "SC", out_ext.rearrange("c p t -> p c t"),
                    accB[:].rearrange("p (c t) -> p c t", c=NCH), [mk_down])
        S.add("gpsimd", lambda e: e.nop(), [("SC", mky)])
        S.emit()
        return nc
    mk_rel2 = S.dma("sync", "SX", vD[:].rearrange("p (c t) -> p c t", c=NCH),
                    xp2.rearrange("c p t -> p c t"), [mk_down])
    acc_marks = [acc_done[0][(c, 1)] for c in range(NCH)]
    mkM4, mkR4 = ln_stats(lambda c: accB[:, c * T:(c + 1) * T], ln4_sq, ln4_st,
                          ln4_M, ln4_R, acc_marks, kind="rms_ln")
    out_half_marks = []
    for c in range(NCH):
        w_ln4 = [("DV", mkR4), ("SX", mk_rel2)]
        if c == 8:
            w_ln4.append(("SX", out_half_marks[0]))
        S.add("vector", lambda e, c=c: e.tensor_tensor(
            out=fscr[:].bitcast(BF16)[:, 0:T], in0=accB[:, c * T:(c + 1) * T],
            in1=ln4_M[:], op=ALU.subtract), w_ln4)
        S.add("vector", lambda e, c=c: e.tensor_tensor(
            out=fsb[:], in0=fscr[:].bitcast(BF16)[:, 0:T], in1=ln4_R[:], op=ALU.mult))
        mk = S.add("vector", lambda e, c=c: e.tensor_tensor(
            out=outB[:, (c % 8) * T:((c % 8) + 1) * T], in0=chunk(vD, c), in1=fsb[:],
            op=ALU.add), inc="DV")
        if c == 7:
            mh0 = S.dma("sync", "SX", out_ext[0:8].rearrange("c p t -> p c t"),
                        outB[:].rearrange("p (c t) -> p c t", c=8), [("DV", mk)])
            out_half_marks.append(mh0)
        elif c == 15:
            mh1 = S.dma("sync", "SX", out_ext[8:16].rearrange("c p t -> p c t"),
                        outB[:].rearrange("p (c t) -> p c t", c=8),
                        [("DV", mk), ("SX", out_half_marks[0])])
            out_half_marks.append(mh1)
        elif c == 8:
            pass
    S.add("sync", lambda e: e.nop(), [("SX", out_half_marks[-1])])

    S.emit()
    return nc


_NC_CACHE = {}


def kernel(L0_attn_x, L0_attn_Wq, L0_attn_bq, L0_attn_Wk, L0_attn_bk,
           L0_attn_Wv, L0_attn_bv, L0_attn_Wo, L0_attn_bo,
           L0_ffn_gate_up_packed, L0_ffn_gate_up_scales,
           L0_ffn_down_packed, L0_ffn_down_scales):
    perm = _perm_stream()
    iperm = _perm_ffn()

    x = np.asarray(L0_attn_x, np.float32)[0]
    xs = np.ascontiguousarray(x[:, perm].T)
    xs = xs.reshape(NCH, 128, NCORES, T)

    def stage_w(W, perm_rows, perm_cols):
        W = np.asarray(W, np.float32)
        if perm_rows is not None:
            W = W[perm_rows, :]
        WT = np.ascontiguousarray(W.T)
        if perm_cols is not None:
            WT = WT[perm_cols, :]
        return np.ascontiguousarray(WT.reshape(NCH, 128, D))

    wq = stage_w(L0_attn_Wq, None, perm)
    wk = stage_w(L0_attn_Wk, None, perm)
    wv = stage_w(L0_attn_Wv, None, perm)
    wo = stage_w(L0_attn_Wo, perm, None)

    bias_pack = np.zeros((128, 64), np.float32)
    bias_pack[:, 0:16] = np.asarray(L0_attn_bq, np.float32).reshape(16, 128).T
    bias_pack[:, 16:32] = np.asarray(L0_attn_bk, np.float32).reshape(16, 128).T
    bias_pack[:, 32:48] = np.asarray(L0_attn_bv, np.float32).reshape(16, 128).T
    bias_pack[:, 48:64] = np.asarray(L0_attn_bo, np.float32)[perm].reshape(16, 128).T

    gu = np.asarray(L0_ffn_gate_up_packed, np.int32)
    col_order = np.empty(GU_COLS, np.int64)
    for b in range(GU_COLS // 512):
        blk = iperm[256 * b:256 * (b + 1)]
        col_order[512 * b:512 * b + 256] = blk
        col_order[512 * b + 256:512 * (b + 1)] = blk + I_FFN
    pkgu_a = np.ascontiguousarray(gu[col_order, :].T).reshape(2, 128, GU_COLS)
    pkgu = np.empty((NSLICE, 128, 1024), np.int32)
    for s in range(NSLICE):
        pkgu[s, :, 0:512] = pkgu_a[0, :, 512 * s:512 * (s + 1)]
        pkgu[s, :, 512:1024] = pkgu_a[1, :, 512 * s:512 * (s + 1)]

    gsc = np.asarray(L0_ffn_gate_up_scales)
    gsc = gsc[:, col_order]
    sgu_a = np.empty((2, 128, 2 * GU_COLS), np.float16)
    for A in range(2):
        sgu_a[A] = np.repeat(np.repeat(gsc[8 * A:8 * (A + 1)], 16, axis=0), 2, axis=1)
    sgu = np.empty((NSLICE, 128, 2048), np.float16)
    for s in range(NSLICE):
        sgu[s, :, 0:1024] = sgu_a[0, :, 1024 * s:1024 * (s + 1)]
        sgu[s, :, 1024:2048] = sgu_a[1, :, 1024 * s:1024 * (s + 1)]

    dp = np.asarray(L0_ffn_down_packed, np.int32)
    dpT = np.ascontiguousarray(dp[perm, :].T)
    pkdn = np.empty((768, D), np.int32)
    pkdn[0:704] = dpT
    pkdn[704:768] = dpT[640:704]
    dsc = np.asarray(L0_ffn_down_scales)
    dsc = dsc[:, perm]
    sdn_full = np.repeat(np.repeat(dsc, 16, axis=0), 2, axis=1)
    sdn = np.empty((768, 2 * D), np.float16)
    sdn[0:704] = sdn_full
    sdn[704:768] = sdn_full[640:704]

    ident = np.eye(128, dtype=np.float32)

    key = "v1"
    if key not in _NC_CACHE:
        _NC_CACHE[key] = build_program()
    nc = _NC_CACHE[key]

    in_maps = []
    for i in range(NCORES):
        cm = np.zeros((8, 1), np.float32)
        cm[:i, 0] = 1.0
        in_maps.append({
            "x_in": np.ascontiguousarray(xs[:, :, i, :]),
            "w_q": wq, "w_k": wk, "w_v": wv, "w_o": wo,
            "bias_pack": bias_pack, "pkgu": pkgu, "sgu": sgu,
            "pkdn": pkdn, "sdn": sdn, "cmask_in": cm, "idin": ident,
        })
    res = run_bass_kernel_spmd(nc, in_maps, core_ids=list(range(NCORES)))

    out = np.empty((8192, D), np.float32)
    for i in range(NCORES):
        o = np.asarray(res.results[i]["out_ext"], np.float32)
        out[T * i:T * (i + 1), :] = o.reshape(D, T).T
    inv = np.empty_like(out)
    inv[:, perm] = out
    return inv.reshape(1, 8192, D)


# version 21
# speedup vs baseline: 1.0406x; 1.0406x over previous
import numpy as np
import ml_dtypes
import concourse.bass as bass
import concourse.mybir as mybir
from concourse.bass_utils import run_bass_kernel_spmd

F32 = mybir.dt.float32
BF16 = mybir.dt.bfloat16
F16 = mybir.dt.float16
I32 = mybir.dt.int32
I16 = mybir.dt.int16
ALU = mybir.AluOpType
AF = mybir.ActivationFunctionType

D = 2048
NCH = 16
T = 1024
NCORES = 8
I_FFN = 5632
GU_COLS = 11264
NSLICE = 22
SCALE = float(128) ** -0.5
EPS_LN = 1e-5
EPS_RMS = 1e-6
KB = 1024

KB_BLOCKS = [(0, 8), (1, 8), (2, 8), (3, 8), (4, 8), (5, 4)]


def _perm_stream():
    p = []
    for A in range(2):
        for j in range(8):
            for cw in range(128):
                p.append(8 * (128 * A + cw) + j)
    return np.array(p, np.int64)


def _perm_ffn():
    p = []
    for cb in range(5):
        for j in range(8):
            for cw in range(128):
                p.append(8 * (128 * cb + cw) + j)
    for a in range(4):
        for c in range(640, 704):
            p.append(8 * c + a)
        for c in range(640, 704):
            p.append(8 * c + a + 4)
    return np.array(p, np.int64)


class Sched:
    ENGINES = ("sync", "gpsimd", "tensor", "scalar", "vector")

    def __init__(self, nc):
        self.nc = nc
        self.ops = {e: [] for e in self.ENGINES}
        self.sems = {}
        self.cnt = {}
        for s in ("SX", "SXL", "SB", "SE", "SW0", "SW1", "SC",
                  "SK0", "SK1", "SG0", "SG1", "SN0", "SN1", "SD0", "SD1",
                  "PE", "AC", "DV", "CC", "GP"):
            self.sems[s] = nc.alloc_semaphore(s)
            self.cnt[s] = 0

    def add(self, eng, fn, waits=(), inc=None, inc_n=1):
        mark = None
        if inc is not None:
            self.cnt[inc] += inc_n
            mark = self.cnt[inc]
        self.ops[eng].append((fn, tuple(waits), inc, inc_n))
        return mark

    def dma(self, eng, sem, out, in_, waits=()):
        return self.add(eng, lambda e, o=out, i=in_: e.dma_start(out=o, in_=i),
                        waits, sem, 16)

    def emit(self):
        with self.nc.Block() as block:
            for eng in self.ENGINES:
                ops = self.ops[eng]

                def body(e, ops=ops):
                    hi = {}
                    for fn, waits, inc, inc_n in ops:
                        for (s, v) in waits:
                            if callable(v):
                                v = v()
                            if hi.get(s, -1) < v:
                                e.wait_ge(self.sems[s], v)
                                hi[s] = v
                        inst = fn(e)
                        if inc is not None:
                            inst.then_inc(self.sems[inc], inc_n)
                getattr(block, eng)(body)


def build_program(debug_tap=None, upto=None):
    nc = bass.Bass("TRN2", target_bir_lowering=False, detect_race_conditions=False)

    x_in = nc.dram_tensor("x_in", [NCH, 128, T], F32, kind="ExternalInput").ap()
    w_q = nc.dram_tensor("w_q", [NCH, 128, D], F32, kind="ExternalInput").ap()
    w_k = nc.dram_tensor("w_k", [NCH, 128, D], F32, kind="ExternalInput").ap()
    w_v = nc.dram_tensor("w_v", [NCH, 128, D], F32, kind="ExternalInput").ap()
    w_o = nc.dram_tensor("w_o", [NCH, 128, D], F32, kind="ExternalInput").ap()
    bias_pack = nc.dram_tensor("bias_pack", [128, 64], F32, kind="ExternalInput").ap()
    pkgu = nc.dram_tensor("pkgu", [NSLICE, 128, 1024], I32, kind="ExternalInput").ap()
    sgu = nc.dram_tensor("sgu", [NSLICE, 128, 2048], F16, kind="ExternalInput").ap()
    pkdn = nc.dram_tensor("pkdn", [768, D], I32, kind="ExternalInput").ap()
    sdn = nc.dram_tensor("sdn", [768, 2 * D], F16, kind="ExternalInput").ap()
    cmask_in = nc.dram_tensor("cmask_in", [8, 1], F32, kind="ExternalInput").ap()
    idin = nc.dram_tensor("idin", [128, 128], F32, kind="ExternalInput").ap()
    out_ext = nc.dram_tensor("out_ext", [NCH, 128, T], F32, kind="ExternalOutput").ap()

    x_park = nc.dram_tensor("x_park", [NCH, 128, T], BF16, kind="Internal").ap()
    xp2 = nc.dram_tensor("xp2", [NCH, 128, T], BF16, kind="Internal").ap()
    cc_in = nc.dram_tensor("cc_in", [1, D], F32, kind="Internal").ap()
    cc_out = nc.dram_tensor("cc_out", [8, D], F32, kind="Internal").ap()

    BASE = (nc.sbuf_base + 31) // 32 * 32
    A0, E0, B0, C0, D0, M0 = (BASE, BASE + 32 * KB, BASE + 64 * KB,
                              BASE + 96 * KB, BASE + 160 * KB, BASE + 192 * KB)
    t_at = nc.alloc_sbuf_tensor_at

    xA = t_at("xA", [128, 16 * T], BF16, offset=A0)
    accF = t_at("accF", [128, 16 * T], F32, offset=A0)
    phiq = t_at("phiq", [128, 16 * T], BF16, offset=E0)
    xE = t_at("xE", [128, 16 * T], BF16, offset=E0)
    xnB = t_at("xnB", [128, 16 * T], BF16, offset=B0)
    wbuf0 = t_at("wbuf0", [128, 16 * T], BF16, offset=C0)
    wbuf1 = t_at("wbuf1", [128, 16 * T], BF16, offset=C0 + 32 * KB)
    vD = t_at("vD", [128, 16 * T], BF16, offset=D0)

    e_scr = t_at("e_scr", [128, 512], F32, offset=M0)
    r_scr = t_at("r_scr", [128, 512], F32, offset=M0 + 2 * KB)
    phi_scr = t_at("phi_scr", [128, 512], BF16, offset=M0 + 4 * KB)
    kv_scr = t_at("kv_scr", [128, T], BF16, offset=M0 + 5 * KB)
    fscr = t_at("fscr", [128, T], F32, offset=M0 + 7 * KB)
    identS = t_at("identS", [128, 128], F32, offset=M0 + 11 * KB)
    biasS = t_at("biasS", [128, 64], F32, offset=M0 + 11 * KB + 512)
    TtS = t_at("TtS", [128, 16], F32, offset=M0 + 11 * KB + 768)
    offsS = t_at("offsS", [128, 16], F32, offset=M0 + 11 * KB + 832)
    cmaskS = t_at("cmaskS", [8, 1], F32, offset=M0 + 11 * KB + 896)
    onesS = t_at("onesS", [128, 1], BF16, offset=M0 + 11 * KB + 928)
    onesRow = t_at("onesRow", [1, 128], F32, offset=M0 + 11 * KB + 960)
    ttrS = t_at("ttrS", [16, 128], F32, offset=M0 + 12 * KB + 448)
    constS = t_at("constS", [1, 8], F32, offset=M0 + 13 * KB)
    conC = t_at("conC", [128, 8], F32, offset=M0 + 13 * KB + 64)

    ln1_M = t_at("ln1_M", [128, T], BF16, offset=D0)
    ln1_R = t_at("ln1_R", [128, T], BF16, offset=D0 + 4 * KB)
    ln1_st = t_at("ln1_st", [1, 2048], F32, offset=D0 + 8 * KB)
    ln1_sq = t_at("ln1_sq", [128, 2 * T], BF16, offset=D0 + 16 * KB)

    g_sb = t_at("g_sb", [8, D], F32, offset=A0)
    mpW = t_at("mpW", [128, 2048], I16, offset=A0 + 8 * KB)
    rrW = t_at("rrW", [128, 2048], BF16, offset=A0 + 12 * KB)
    pkgW = t_at("pkgW", [128, 1024], I32, offset=A0 + 16 * KB)
    sxgW = t_at("sxgW", [128, 2048], F16, offset=A0 + 20 * KB)
    rrW2 = t_at("rrW2", [128, 2048], BF16, offset=A0 + 28 * KB)
    ln2_M = t_at("ln2_M", [128, T], BF16, offset=A0 + 8 * KB)
    ln2_R = t_at("ln2_R", [128, T], BF16, offset=A0 + 12 * KB)
    ln2_st = t_at("ln2_st", [1, 2048], F32, offset=A0 + 16 * KB)
    ln2_sq = t_at("ln2_sq", [128, 2 * T], BF16, offset=A0 + 24 * KB)
    e_scr2 = t_at("e_scr2", [128, 512], F32, offset=A0 + 28 * KB)
    r_scr2 = t_at("r_scr2", [128, 512], F32, offset=A0 + 30 * KB)
    phi_scr2 = t_at("phi_scr2", [128, 512], BF16, offset=A0 + 14 * KB)

    ln3_sq = t_at("ln3_sq", [128, 2 * T], BF16, offset=A0)
    ln3_M = t_at("ln3_M", [128, T], BF16, offset=A0 + 4 * KB)
    ln3_R = t_at("ln3_R", [128, T], BF16, offset=A0 + 6 * KB)
    ln3_st = t_at("ln3_st", [1, 2048], F32, offset=A0 + 8 * KB)

    pkg = t_at("pkg", [128, 2048], I32, offset=C0)
    sxg = t_at("sxg", [128, 4096], F16, offset=C0 + 8 * KB)
    lhsgu = t_at("lhsgu", [128, 16384], BF16, offset=C0 + 16 * KB)
    MPT = t_at("MPT", [128, 2048], I16, offset=C0 + 48 * KB)
    RRT = t_at("RRT", [128, 2048], BF16, offset=C0 + 52 * KB)
    pkdb = t_at("pkdb", [128, 1024], I32, offset=C0 + 56 * KB)
    sxdb = t_at("sxdb", [128, 2048], F16, offset=C0 + 60 * KB)
    A2T = t_at("A2T", [128, 1024], I32, offset=M0 + 7 * KB)
    SMT = t_at("SMT", [128, 2048], I16, offset=M0)
    lhsdn = t_at("lhsdn", [128, 16384], BF16, offset=E0)
    lhstl = t_at("lhstl", [128, 8192], BF16, offset=E0)
    accB = t_at("accB", [128, 16 * T], BF16, offset=A0)
    outB = t_at("outB", [128, 8 * T], F32, offset=B0)
    ln4_st = t_at("ln4_st", [1, 3072], F32, offset=C0 + 16 * KB)
    ln4_M = t_at("ln4_M", [128, T], BF16, offset=C0)
    ln4_R = t_at("ln4_R", [128, T], BF16, offset=C0 + 2 * KB)
    ln4_sq = t_at("ln4_sq", [128, 2 * T], BF16, offset=C0 + 4 * KB)
    ln4_u = None
    fsb = t_at("fsb", [128, T], BF16, offset=M0 + 5 * KB)

    psA = nc.alloc_psum_tensor("psA", [128, 2048], F32)
    psB = nc.alloc_psum_tensor("psB", [128, 2048], F32)

    S = Sched(nc)

    def _finish(buf, waits=()):
        mk = S.dma("gpsimd", "SC", out_ext.rearrange("c p t -> p c t"),
                   buf[:].rearrange("p (c t) -> p c t", c=NCH), list(waits))
        S.add("gpsimd", lambda e: e.nop(), [("SC", mk)])
        S.emit()
        return nc

    def chunk(t, c, lo=0, hi=T):
        return t[:, c * T + lo: c * T + hi]

    S.add("vector", lambda e: e.memset(onesS[:], 1.0))
    S.add("vector", lambda e: e.memset(onesRow[:], 1.0))
    S.add("vector", lambda e: e.memset(constS[0:1, 0:1], EPS_LN))
    S.add("vector", lambda e: e.memset(constS[0:1, 1:2], -0.5))
    S.add("vector", lambda e: e.memset(constS[0:1, 2:3], 1.0 / D))
    S.add("vector", lambda e: e.memset(conC[:, 0:1], -1.0))
    S.add("vector", lambda e: e.memset(conC[:, 1:2], 16128.0), inc="DV")
    mk_ones = S.cnt["DV"]
    mk_bias = S.dma("sync", "SB", biasS[:], bias_pack)
    mk_id = S.dma("sync", "SB", identS[:], idin)
    mk_cm = S.dma("sync", "SB", cmaskS[:], cmask_in)
    mk_x = S.dma("gpsimd", "SXL", xA[:].rearrange("p (c t) -> p c t", c=NCH),
                 x_in.rearrange("c p t -> p c t"))
    wv0 = S.dma("gpsimd", "SW0", wbuf0[:].rearrange("p (c e) -> p c e", c=NCH),
                w_v[:, :, 0:1024].rearrange("c p e -> p c e"))
    wv1 = S.dma("gpsimd", "SW1", wbuf1[:].rearrange("p (c e) -> p c e", c=NCH),
                w_v[:, :, 1024:2048].rearrange("c p e -> p c e"))

    def ln_stats(src_chunk, sq_t, st, M_t, R_t, mk_src, kind="ln", cp_t=None):
        pe_marks = []
        for c in range(NCH):
            half = (c % 2)
            sqv = sq_t[:, half * T:(half + 1) * T]
            waits = [mk_src[c]] if isinstance(mk_src, list) else [mk_src]
            if c >= 2:
                waits = waits + [("PE", pe_marks[c - 2])]
            if kind == "rms_ln" and cp_t is not None:
                cpv = cp_t[:, half * T:(half + 1) * T]
                S.add("scalar", lambda e, s=src_chunk(c), d=cpv: e.activation(
                    out=d, in_=s, func=AF.Identity), waits)
            mkq = S.add("scalar", lambda e, s=src_chunk(c), d=sqv: e.activation(
                out=d, in_=s, func=AF.Square), waits, "AC")
            rhs_sum = (src_chunk(c) if (kind != "rms_ln" or cp_t is None)
                       else cp_t[:, half * T:(half + 1) * T])
            first, last = (c == 0), (c == NCH - 1)
            for (bank, rh) in ((0, rhs_sum), (1, sqv)):
                for n in range(2):
                    mk = S.add("tensor", lambda e, b=bank, n=n, r=rh, f=first, l=last: e.matmul(
                        psA[0:1, (2 * b + n) * 512:(2 * b + n + 1) * 512],
                        onesS[:], r[:, n * 512:(n + 1) * 512],
                        start=f, stop=l),
                        [("AC", mkq), (("DV", mk_ones))], "PE")
            pe_marks.append(mk)
        mk_st = mk
        S.add("scalar", lambda e: e.activation(out=st[0:1, 0:512], in_=psA[0:1, 0:512],
                                               func=AF.Identity, scale=constS[0:1, 2:3]),
              [("PE", mk_st), ("DV", mk_ones)])
        S.add("scalar", lambda e: e.activation(out=st[0:1, 512:1024], in_=psA[0:1, 512:1024],
                                               func=AF.Identity, scale=constS[0:1, 2:3]))
        S.add("scalar", lambda e: e.activation(out=st[0:1, 1024:1536], in_=psA[0:1, 1024:1536],
                                               func=AF.Identity, scale=constS[0:1, 2:3]))
        mk_mu = S.add("scalar", lambda e: e.activation(out=st[0:1, 1536:2048], in_=psA[0:1, 1536:2048],
                                                       func=AF.Identity, scale=constS[0:1, 2:3]), inc="AC")
        mu = st[0:1, 0:1024]
        m2 = st[0:1, 1024:2048]
        for n in range(2):
            mkbM = S.add("tensor", lambda e, n=n: e.matmul(
                psB[:, n * 512:(n + 1) * 512], onesRow[:], mu[:, n * 512:(n + 1) * 512],
                start=True, stop=True), [("AC", mk_mu)], "PE")
        mkM = S.add("vector", lambda e: e.tensor_copy(out=M_t[:], in_=psB[:, 0:1024]),
                    [("PE", mkbM)], "DV")
        mk1 = S.add("vector", lambda e: e.tensor_tensor(
            out=psA[0:1, 0:1024], in0=mu, in1=mu, op=ALU.mult), [("DV", mkM)], "DV")
        if kind == "ln":
            mkv = S.add("vector", lambda e: e.tensor_tensor(
                out=st[0:1, 0:1024], in0=m2, in1=psA[0:1, 0:1024], op=ALU.subtract),
                [("DV", mk1)], "DV")
            eps = EPS_LN
        else:
            mkd = S.add("vector", lambda e: e.tensor_scalar(
                out=st[0:1, 2048:3072], in0=m2, scalar1=1.0 + EPS_LN,
                scalar2=EPS_LN * EPS_RMS, op0=ALU.mult, op1=ALU.add), [("DV", mk1)], "DV")
            mkv = S.add("vector", lambda e: e.tensor_tensor(
                out=st[0:1, 0:1024], in0=st[0:1, 2048:3072], in1=psA[0:1, 0:1024],
                op=ALU.subtract), [("DV", mkd)], "DV")
            eps = 0.0
        bias_ap = constS[0:1, 0:1] if eps else 0.0
        mkl = S.add("scalar", lambda e, b=bias_ap: e.activation(
            out=st[0:1, 1024:2048], in_=st[0:1, 0:1024], func=AF.Ln, bias=b),
            [("DV", mkv)], "AC")
        mke = S.add("scalar", lambda e: e.activation(
            out=st[0:1, 0:1024], in_=st[0:1, 1024:2048], func=AF.Exp,
            scale=constS[0:1, 1:2]),
            [("AC", mkl)], "AC")
        for n in range(2):
            mkbR = S.add("tensor", lambda e, n=n: e.matmul(
                psB[:, (2 + n) * 512:(3 + n) * 512], onesRow[:],
                st[0:1, n * 512:(n + 1) * 512], start=True, stop=True),
                [("AC", mke)], "PE")
        mkR = S.add("vector", lambda e: e.tensor_copy(out=R_t[:], in_=psB[:, 1024:2048]),
                    [("PE", mkbR)], "DV")
        return mkM, mkR

    mk_xc = mk_x
    mkM1, mkR1 = ln_stats(lambda c: chunk(xA, c), ln1_sq, ln1_st, ln1_M, ln1_R,
                          [("SXL", mk_x)] * NCH)
    mk_xn1 = []
    for c in range(NCH):
        S.add("vector", lambda e, c=c: e.tensor_tensor(
            out=fscr[:].bitcast(BF16)[:, 0:T], in0=chunk(xA, c), in1=ln1_M[:], op=ALU.subtract),
            [("DV", mkR1)])
        mk = S.add("vector", lambda e, c=c: e.tensor_tensor(
            out=chunk(xnB, c), in0=fscr[:].bitcast(BF16)[:, 0:T], in1=ln1_R[:], op=ALU.mult), inc="DV")
        mk_xn1.append(("DV", mk))
    mk_xn1_all = mk_xn1[-1]

    if upto == 'xn1':
        return _finish(xnB, [mk_xn1_all])
    mk_park1 = S.dma("sync", "SX", x_park.rearrange("c p t -> p c t"),
                     xA[:].rearrange("p (c t) -> p c t", c=NCH), [mk_xn1_all])

    bank_users = {0: [], 1: []}

    def proj(wname, wmarks, evac, next_load=None, after_group=None, rhs_marks=None):
        g = 0
        for mh in range(2):
            wb = wbuf0 if mh == 0 else wbuf1
            last_mm = None
            for mg in range(2):
                for n in range(2):
                    bank = g % 2
                    ps = psA if bank == 0 else psB
                    waits = [(("SW%d" % mh), wmarks[mh])] + bank_users[bank]
                    if rhs_marks is None:
                        waits = waits + [mk_xn1_all]
                    for mi in range(4):
                        m_glob = mh * 8 + mg * 4 + mi
                        for k in range(NCH):
                            w_ = waits if (mi == 0 and k == 0) else ()
                            if rhs_marks is not None and g == 0 and mi == 0:
                                w_ = list(w_) + [rhs_marks[k]]
                            last_mm = S.add("tensor", lambda e, wb=wb, k=k, mg=mg, mi=mi, n=n, ps=ps: e.matmul(
                                ps[:, mi * 512:(mi + 1) * 512],
                                wb[:, k * 1024 + (mg * 4 + mi) * 128: k * 1024 + (mg * 4 + mi + 1) * 128],
                                chunk(xnB, k, n * 512, (n + 1) * 512),
                                start=(k == 0), stop=(k == NCH - 1)),
                                w_,
                                "PE" if (mi == 3 and k == NCH - 1) else None)
                    mm_mark = ("PE", last_mm)
                    rel = []
                    for mi in range(4):
                        m_glob = mh * 8 + mg * 4 + mi
                        rel += evac(g, m_glob, ps, ps[:, mi * 512:(mi + 1) * 512], n, mm_mark)
                    bank_users[bank] = rel
                    if after_group is not None:
                        after_group(g)
                    g += 1
            if next_load is not None:
                next_load(mh, ("PE", last_mm))

    wk_marks = [None, None]

    def load_wk(mh, after):
        src = w_k[:, :, mh * 1024:(mh + 1) * 1024].rearrange("c p e -> p c e")
        dst = (wbuf0 if mh == 0 else wbuf1)[:].rearrange("p (c e) -> p c e", c=NCH)
        wk_marks[mh] = S.dma("gpsimd", "SW%d" % mh, dst, src, [after])

    def evac_v(g, m, ps, tile, n, mm_mark):
        mk = S.add("scalar", lambda e, m=m, tile=tile, n=n: e.activation(
            out=chunk(vD, m, n * 512, (n + 1) * 512), in_=tile,
            func=AF.Identity, bias=biasS[:, 32 + m:33 + m]),
            [mm_mark, ("SB", 48)], "AC")
        return [("AC", mk)]

    proj("wv", (wv0, wv1), evac_v, load_wk, rhs_marks=mk_xn1)
    mk_v = S.cnt["AC"]
    if upto == 'v':
        return _finish(vD, [('AC', mk_v)])

    wq_marks = [None, None]

    def load_wq(mh, after):
        src = w_q[:, :, mh * 1024:(mh + 1) * 1024].rearrange("c p e -> p c e")
        dst = (wbuf0 if mh == 0 else wbuf1)[:].rearrange("p (c e) -> p c e", c=NCH)
        wq_marks[mh] = S.dma("gpsimd", "SW%d" % mh, dst, src, [after])

    scan_done = [None] * NCH
    ev_state = {"dv_last": None, "ctr": 0, "dv_par": [None, None]}

    def evac_k(g, m, ps, tile, n, mm_mark):
        par = ev_state["ctr"] % 2
        ev_state["ctr"] += 1
        es_, rs_, phs_ = ((e_scr, r_scr, phi_scr) if par == 0
                          else (e_scr2, r_scr2, phi_scr2))
        w0 = [mm_mark, ("SB", 48)]
        if ev_state["dv_par"][par] is not None:
            w0 = w0 + [("DV", ev_state["dv_par"][par])]
        S.add("scalar", lambda e, tile=tile, es_=es_: e.activation(
            out=es_[:], in_=tile, func=AF.Exp, bias=biasS[:, 16 + m:17 + m]), w0)
        mke = S.add("scalar", lambda e, tile=tile, rs_=rs_: e.activation(
            out=rs_[:], in_=tile, func=AF.Relu, bias=biasS[:, 16 + m:17 + m]), inc="AC")
        S.add("vector", lambda e, es_=es_, rs_=rs_, phs_=phs_: e.scalar_tensor_tensor(
            out=phs_[:], in0=es_[:], scalar=1.0, in1=rs_[:],
            op0=ALU.min, op1=ALU.add), [("AC", mke)])
        mkv2 = S.add("vector", lambda e, m=m, n=n, phs_=phs_: e.scalar_tensor_tensor(
            out=chunk(vD, m, n * 512, (n + 1) * 512), in0=phs_[:], scalar=SCALE,
            in1=chunk(vD, m, n * 512, (n + 1) * 512),
            op0=ALU.mult, op1=ALU.mult), inc="DV")
        ev_state["dv_par"][par] = mkv2
        ev_state["last_ac"] = mke
        if n == 1:
            mksc = S.add("vector", lambda e, m=m: e.tensor_tensor_scan(
                out=fscr[:], data0=chunk(vD, m), data1=chunk(vD, m), initial=0.0,
                op0=ALU.add, op1=ALU.bypass), (), "DV")
            S.add("vector", lambda e, m=m: e.tensor_copy(
                out=chunk(vD, m), in_=fscr[:]), [("DV", mksc)])
            mk3 = S.add("vector", lambda e, m=m: e.tensor_copy(
                out=TtS[:, m:m + 1], in_=fscr[:, T - 1:T]), (), "DV")
            scan_done[m] = mk3
            ccs = "SB" if m % 2 == 0 else "SE"
            cc_dma[m % 2] = (ccs, S.dma("sync", ccs,
                             cc_in.rearrange("a (c p) -> (a p) c", c=16)[:, m:m + 1],
                             TtS[:, m:m + 1], [("DV", mk3)]))
        return [("AC", ev_state["last_ac"])]

    cc_dma = [None, None]
    proj("wk", wk_marks, evac_k, load_wq)
    mk_scans_pre = ("DV", scan_done[NCH - 1])
    mk_coll = S.add("gpsimd", lambda e: e.collective_compute(
        "AllGather", ALU.bypass, replica_groups=[list(range(NCORES))],
        ins=[cc_in.opt()], outs=[cc_out.opt()]),
        [(cc_dma[0][0], cc_dma[0][1]), (cc_dma[1][0], cc_dma[1][1])], "CC")
    mk_gsb = S.dma("gpsimd", "SC", g_sb[:], cc_out,
                   [("CC", mk_coll), ("SX", mk_park1)])

    if upto == 'kvs':
        return _finish(vD, [mk_scans_pre])
    wo_marks = [None, None]

    def load_wo(mh, after):
        src = w_o[:, :, mh * 1024:(mh + 1) * 1024].rearrange("c p e -> p c e")
        dst = (wbuf0 if mh == 0 else wbuf1)[:].rearrange("p (c e) -> p c e", c=NCH)
        wo_marks[mh] = S.dma("gpsimd", "SW%d" % mh, dst, src, [after])

    def evac_q(g, m, ps, tile, n, mm_mark):
        par = ev_state["ctr"] % 2
        ev_state["ctr"] += 1
        es_, rs_ = (e_scr, r_scr) if par == 0 else (e_scr2, r_scr2)
        w0 = [mm_mark, ("SB", 48)]
        if ev_state["dv_par"][par] is not None:
            w0 = w0 + [("DV", ev_state["dv_par"][par])]
        S.add("scalar", lambda e, tile=tile, es_=es_: e.activation(
            out=es_[:], in_=tile, func=AF.Exp, bias=biasS[:, m:m + 1]), w0)
        mke = S.add("scalar", lambda e, tile=tile, rs_=rs_: e.activation(
            out=rs_[:], in_=tile, func=AF.Relu, bias=biasS[:, m:m + 1]), inc="AC")
        mkp = S.add("vector", lambda e, m=m, n=n, es_=es_, rs_=rs_: e.scalar_tensor_tensor(
            out=chunk(phiq, m, n * 512, (n + 1) * 512), in0=es_[:], scalar=1.0,
            in1=rs_[:], op0=ALU.min, op1=ALU.add), [("AC", mke)], "DV")
        ev_state["dv_par"][par] = mkp
        if n == 1 and m >= 12:
            mkc = S.add("vector", lambda e, m=m: e.scalar_tensor_tensor(
                out=chunk(vD, m), in0=chunk(vD, m), scalar=offsS[:, m:m + 1],
                in1=chunk(phiq, m), op0=ALU.add, op1=ALU.mult),
                [("DV", offs_mark[0])], "DV")
            ctx_marks.append(("DV", mkc))
        return [("AC", mke)]

    ev_state["dv_par"] = [None, None]
    ctx_marks = []
    offs_mark = [None]

    def defer_ctx(ms):
        for m in ms:
            mkc = S.add("vector", lambda e, m=m: e.scalar_tensor_tensor(
                out=chunk(vD, m), in0=chunk(vD, m), scalar=offsS[:, m:m + 1],
                in1=chunk(phiq, m), op0=ALU.add, op1=ALU.mult),
                [("DV", offs_mark[0])], "DV")
            ctx_marks.append(("DV", mkc))

    def after_wq_group(g):
        if g == 6:
            for c in range(NCH):
                mk_omm = S.add("tensor", lambda e, c=c: e.matmul(
                    psB[:, c:c + 1], g_sb[:, c * 128:(c + 1) * 128], cmaskS[:],
                    start=True, stop=True),
                    ([("SC", mk_gsb), ("SB", 48)] + bank_users[1]) if c == 0 else (),
                    "PE" if c == NCH - 1 else None)
            mko = S.add("vector", lambda e: e.tensor_copy(
                out=offsS[:], in_=psB[:, 0:16]), [("PE", mk_omm)], "DV")
            offs_mark[0] = mko
            bank_users[1] = bank_users[1] + [("DV", mko)]
            defer_ctx(range(0, 12))

    proj("wq", wq_marks, evac_q, load_wo, after_group=after_wq_group)
    mk_phiq = ("DV", S.cnt["DV"])

    if upto == 'phiq':
        return _finish(phiq, [mk_phiq])
    mk_ctx_all = ctx_marks[-1]

    mk_rel1 = S.dma("sync", "SX", xE[:].rearrange("p (c t) -> p c t", c=NCH),
                    x_park.rearrange("c p t -> p c t"), [mk_ctx_all])

    scr_a2_free = [None, None]
    scr_mp_free = [None, None]
    scr_rr_free = [None, None]
    a2_marks = [None, None]
    act_marks = [None, None]

    def dq_extract(par, jp, wslab, extra_waits=()):
        a2 = A2T[:, par * 512:(par + 1) * 512]
        sh = 6 - 4 * jp
        op = ALU.logical_shift_left if sh >= 0 else ALU.logical_shift_right
        w0 = list(extra_waits)
        if scr_a2_free[par] is not None:
            w0.append(scr_a2_free[par])
        mka2 = S.add("vector", lambda e, a2=a2, w=wslab, sh=abs(sh), op=op: e.tensor_scalar(
            out=a2, in0=w, scalar1=sh, scalar2=0x01C001C0,
            op0=op, op1=ALU.bitwise_and), w0, "DV")
        shs = 12 - 4 * jp
        S.add("vector", lambda e, w=wslab, shs=shs, par=par: e.tensor_scalar(
            out=SMT[:, par * 1024:(par + 1) * 1024].bitcast(I32), in0=w, scalar1=shs,
            scalar2=0x80008000 - (1 << 32),
            op0=ALU.logical_shift_left, op1=ALU.bitwise_and))
        a2_marks[par] = mka2

    def dq_act(par):
        a2 = A2T[:, par * 512:(par + 1) * 512]
        mp = MPT[:, par * 1024:(par + 1) * 1024]
        rr = RRT[:, par * 1024:(par + 1) * 1024]
        wmp = [("DV", a2_marks[par])]
        if scr_mp_free[par] is not None:
            wmp.append(scr_mp_free[par])
        mkmp = S.add("scalar", lambda e, a2=a2, mp=mp: e.activation(
            out=mp, in_=a2.bitcast(I16), func=AF.Identity, bias=conC[:, 1:2]),
            wmp, "AC")
        wrr = [("AC", mkmp)]
        if scr_rr_free[par] is not None:
            wrr.append(scr_rr_free[par])
        mkr = S.add("scalar", lambda e, mp=mp, rr=rr: e.activation(
            out=rr, in_=mp.bitcast(BF16), func=AF.Relu,
            scale=conC[:, 0:1], bias=1.0), wrr, "AC")
        scr_a2_free[par] = ("AC", mkmp)
        act_marks[par] = mkr

    def dq_consume(par, sxslab, out_ap, tail_a=None, tailtile=None):
        mp = MPT[:, par * 1024:(par + 1) * 1024]
        rr = RRT[:, par * 1024:(par + 1) * 1024]
        smt = SMT[:, par * 1024:(par + 1) * 1024]
        mksub = S.add("vector", lambda e, mp=mp, rr=rr: e.tensor_tensor(
            out=rr, in0=mp.bitcast(BF16), in1=rr, op=ALU.subtract),
            [("AC", act_marks[par])], "DV")
        mkor = S.add("vector", lambda e, rr=rr, smt=smt: e.tensor_tensor(
            out=smt, in0=rr.bitcast(I16), in1=smt, op=ALU.bitwise_or), (), "DV")
        scr_mp_free[par] = ("DV", mksub)
        scr_rr_free[par] = ("DV", mkor)
        vs = smt.bitcast(BF16)
        if tail_a is None:
            mkmul = S.add("vector", lambda e, vs=vs, sx=sxslab, o=out_ap: e.tensor_tensor(
                out=o, in0=vs, in1=sx, op=ALU.mult), (), "DV")
        else:
            vs3 = vs.rearrange("p (n two) -> p n two", two=2)
            sx3 = sxslab.rearrange("p (n two) -> p n two", two=2)
            S.add("vector", lambda e, vs3=vs3, sx3=sx3, t=tailtile: e.tensor_tensor(
                out=t[0:64, :], in0=vs3[0:64, :, 0], in1=sx3[0:64, :, 0], op=ALU.mult))
            mkmul = S.add("vector", lambda e, vs3=vs3, sx3=sx3, t=tailtile: e.tensor_tensor(
                out=t[64:128, :], in0=vs3[64:128, :, 1], in1=sx3[64:128, :, 1],
                op=ALU.mult), (), "DV")
        return ("DV", mkmul)

    mpW_free = [None, None]
    vsgnW_free = [None] * 4
    actW_marks = [None, None]
    wv_ctr = [0]

    def dqW_act(par):
        a2 = A2T[:, par * 512:(par + 1) * 512]
        mp = mpW[:, par * 1024:(par + 1) * 1024]
        wmp = [("DV", a2_marks[par])]
        if mpW_free[par] is not None:
            wmp.append(mpW_free[par])
        mkmp = S.add("scalar", lambda e, a2=a2, mp=mp: e.activation(
            out=mp, in_=a2.bitcast(I16), func=AF.Identity, bias=conC[:, 1:2]),
            wmp, "AC")
        scr_a2_free[par] = ("AC", mkmp)
        actW_marks[par] = mkmp

    def dqW_consume(par, sxslab, out_ap, gate):
        mp = mpW[:, par * 1024:(par + 1) * 1024]
        smt = SMT[:, par * 1024:(par + 1) * 1024]
        vs = wv_ctr[0] % 4
        wv_ctr[0] += 1
        slot = (rrW if vs < 2 else rrW2)[:, (vs % 2) * 1024:(vs % 2 + 1) * 1024]
        wt = [("AC", actW_marks[par])]
        if vsgnW_free[vs] is not None:
            wt.append(vsgnW_free[vs])
        S.add("vector", lambda e, mp=mp, t1=slot: e.tensor_scalar(
            out=t1, in0=mp.bitcast(BF16), scalar1=2.0, scalar2=-1.0,
            op0=ALU.mult, op1=ALU.add), wt)
        mkmin = S.add("vector", lambda e, mp=mp, t1=slot: e.tensor_tensor(
            out=t1, in0=t1, in1=mp.bitcast(BF16), op=ALU.min), (), "DV")
        mkor = S.add("vector", lambda e, t1=slot, smt=smt: e.tensor_tensor(
            out=t1.bitcast(I16), in0=t1.bitcast(I16), in1=smt,
            op=ALU.bitwise_or), (), "DV")
        mpW_free[par] = ("DV", mkmin)
        mkmul = S.add("gpsimd", lambda e, t1=slot, sx=sxslab, o=out_ap: e.tensor_tensor(
            out=o, in0=t1.bitcast(BF16), in1=sx, op=ALU.mult),
            [gate, ("DV", mkor)], "GP")
        vsgnW_free[vs] = ("GP", mkmul)
        return ("GP", mkmul)

    def dqW_unit_gen(s, gate):
        sp = s % 2
        specs = []
        for A in range(2):
            for jpp in range(4):
                pp = A * 4 + jpp
                wsl = (pkgW[:, A * 512:(A + 1) * 512] if s == 0 else
                       pkg[:, sp * 1024 + A * 512: sp * 1024 + (A + 1) * 512])
                xsl = (sxgW[:, A * 1024:(A + 1) * 1024] if s == 0 else
                       sxg[:, sp * 2048 + A * 1024: sp * 2048 + A * 1024 + 1024])
                specs.append((jpp, wsl, xsl,
                              lhsgu[:, sp * 8192 + pp * 1024: sp * 8192 + (pp + 1) * 1024]))
        n = len(specs)
        pars = [None] * n
        last = None
        for i in range(n + 2):
            if i >= 2:
                jp, w, sx, out = specs[i - 2]
                last = dqW_consume(pars[i - 2], sx, out, gate)
            if i < n:
                jp, w, sx, out = specs[i]
                par = chain_par[0] % 2
                chain_par[0] += 1
                pars[i] = par
                ew = [("SK%d" % sp, pk_marks[s]), ("SG%d" % sp, sx_marks[s])] if i == 0 else ()
                dq_extract(par, jp, w, ew)
                dqW_act(par)
            yield
        dq_marks[("gu", s)] = last

    def lgu_view(buf, A, j, mi):
        pp = A * 4 + (j % 4)
        lane = j // 4
        t3 = lhsgu[:, buf * 8192 + pp * 1024: buf * 8192 + (pp + 1) * 1024]
        return t3.rearrange("p (n two) -> p n two", two=2)[:, mi * 128:(mi + 1) * 128, lane]

    def ldn_view(j, m):
        pp = j % 4
        lane = j // 4
        t3 = lhsdn[:, pp * 4096:(pp + 1) * 4096]
        return t3.rearrange("p (n two) -> p n two", two=2)[:, m * 128:(m + 1) * 128, lane]

    lhsgu_free = [None, None]
    sil_user = [None, None]
    pk_marks = [None] * NSLICE
    sx_marks = [None] * NSLICE
    acc_init = {}
    hblk_marks = [None, None]
    lhsdn_free = [None]
    chain_par = [0]
    dq_marks = {}
    hs_ready = [None]

    def issue_gu_dma(s, waits):
        pk_marks[s] = S.dma("sync", "SK%d" % (s % 2),
                            pkg[:, (s % 2) * 1024:(s % 2 + 1) * 1024],
                            pkgu[s], waits)
        sx_marks[s] = S.dma("sync", "SG%d" % (s % 2),
                            sxg[:, (s % 2) * 2048:(s % 2 + 1) * 2048],
                            sgu[s], waits)

    def dq_unit_gen(u):
        if u[0] == "gu":
            s = u[1]
            sp = s % 2
            specs = []
            for A in range(2):
                for jpp in range(4):
                    pp = A * 4 + jpp
                    specs.append(dict(
                        jp=jpp,
                        w=pkg[:, sp * 1024 + A * 512: sp * 1024 + (A + 1) * 512],
                        sx=sxg[:, sp * 2048 + A * 1024: sp * 2048 + A * 1024 + 1024],
                        out=lhsgu[:, sp * 8192 + pp * 1024: sp * 8192 + (pp + 1) * 1024],
                        waits=[("SK%d" % sp, pk_marks[s]), ("SG%d" % sp, sx_marks[s])],
                        tail=None))
            specs[0]["waits"] = list(specs[0]["waits"])
            if lhsgu_free[sp] is not None:
                specs[0]["waits"].append(lhsgu_free[sp])
            key = ("gu", s)
            dma_cb = None
        else:
            kbi, cb = u[1], u[2]
            tailmode = (cb == 5)
            specs = []
            dmarks = {}

            def issue_dn_dma(cs, extra):
                cp = cs % 2
                mkp = S.dma("sync", "SN%d" % cp, pkdb[:, cp * 512:(cp + 1) * 512],
                            pkdn[128 * cb:128 * (cb + 1), 512 * cs:512 * (cs + 1)], extra)
                mks = S.dma("sync", "SD%d" % cp, sxdb[:, cp * 1024:(cp + 1) * 1024],
                            sdn[128 * cb:128 * (cb + 1), 1024 * cs:1024 * (cs + 1)], extra)
                dmarks[cs] = [("SN%d" % cp, mkp), ("SD%d" % cp, mks)]
            w0 = [lhsdn_free[0]] if lhsdn_free[0] is not None else []
            issue_dn_dma(0, w0)
            issue_dn_dma(1, w0)
            for cs in range(4):
                cp = cs % 2
                for jp in range(4):
                    specs.append(dict(
                        jp=jp,
                        w=pkdb[:, cp * 512:(cp + 1) * 512],
                        sx=sxdb[:, cp * 1024:(cp + 1) * 1024],
                        out=(None if tailmode else
                             lhsdn[:, jp * 4096 + cs * 1024: jp * 4096 + (cs + 1) * 1024]),
                        waits=None, dmarks=dmarks,
                        tail=(lhstl[:, jp * 2048 + cs * 512: jp * 2048 + (cs + 1) * 512]
                              if tailmode else None),
                        cs=cs))
            key = ("dn", kbi)
            dma_cb = issue_dn_dma

        n = len(specs)
        pars = [None] * n
        last = None
        for i in range(n + 2):
            if i >= 2:
                sp_ = specs[i - 2]
                par = pars[i - 2]
                if sp_["tail"] is None:
                    last = dq_consume(par, sp_["sx"], sp_["out"])
                else:
                    last = dq_consume(par, sp_["sx"], None, tail_a=sp_["jp"],
                                      tailtile=sp_["tail"])
                if dma_cb is not None and (i - 2) % 4 == 3:
                    cs_done = (i - 2) // 4
                    if cs_done + 2 < 4:
                        dma_cb(cs_done + 2, [last])
            if i < n:
                sp_ = specs[i]
                par = chain_par[0] % 2
                chain_par[0] += 1
                pars[i] = par
                ew = (sp_["waits"] if sp_.get("waits") is not None
                      else sp_["dmarks"][sp_["cs"]])
                dq_extract(par, sp_["jp"], sp_["w"], ew)
                dq_act(par)
            yield
        dq_marks[key] = last

    def drain(g):
        for _ in g:
            pass

    def lgu_view(buf, A, j, mi):
        pp = A * 4 + (j % 4)
        lane = j // 4
        t3 = lhsgu[:, buf * 8192 + pp * 1024: buf * 8192 + (pp + 1) * 1024]
        return t3.rearrange("p (n two) -> p n two", two=2)[:, mi * 128:(mi + 1) * 128, lane]

    def ldn_view(j, m):
        pp = j % 4
        lane = j // 4
        t3 = lhsdn[:, pp * 4096:(pp + 1) * 4096]
        return t3.rearrange("p (n two) -> p n two", two=2)[:, m * 128:(m + 1) * 128, lane]

    def gen_mm_gu(s):
        sp = s % 2
        last_mm = None
        for n in range(2):
            ps = psA if n == 0 else psB
            waits = [dq_marks[("gu", s)], mk_xn3_all] + bank_users[n]
            for mi in range(4):
                for A in range(2):
                    for j in range(8):
                        k = A * 8 + j
                        last_mm = S.add("tensor", lambda e, A=A, j=j, mi=mi, n=n, ps=ps, k=k, sp=sp: e.matmul(
                            ps[:, mi * 512:(mi + 1) * 512],
                            lgu_view(sp, A, j, mi),
                            chunk(xnB, k, n * 512, (n + 1) * 512),
                            start=(k == 0), stop=(k == 15)),
                            waits if (mi == 0 and k == 0) else (),
                            "PE" if (mi == 3 and k == 15) else None)
            dq_marks[("gumm", s, n)] = ("PE", last_mm)
        lhsgu_free[sp] = ("PE", last_mm)

    def gen_ev_gu(s, kbi, si):
        buf = kbi % 2
        hbase = buf * 8
        rel_all = []
        for n in range(2):
            ps = psA if n == 0 else psB
            mm_mark = dq_marks[("gumm", s, n)]
            rel = []
            for half in range(2):
                hs_chunk = hbase + 2 * si + half
                silv = kv_scr[:, half * 512:(half + 1) * 512]
                wv_ = [mm_mark]
                if sil_user[half] is not None:
                    wv_ = wv_ + [sil_user[half]]
                if si == 0 and n == 0 and half == 0:
                    if hblk_marks[buf] is not None:
                        wv_ = wv_ + [hblk_marks[buf]]
                    if kbi == 0:
                        wv_ = wv_ + [("SX", mk_park2)]
                mksil = S.add("scalar", lambda e, ps=ps, half=half, silv=silv: e.activation(
                    out=silv, in_=ps[:, half * 512:(half + 1) * 512], func=AF.Silu),
                    wv_, "AC")
                mkh = S.add("vector", lambda e, ps=ps, half=half, silv=silv, hs=hs_chunk, n=n: e.tensor_tensor(
                    out=chunk(vD, hs, n * 512, (n + 1) * 512), in0=silv,
                    in1=ps[:, (2 + half) * 512:(3 + half) * 512], op=ALU.mult),
                    [("AC", mksil)], "DV")
                sil_user[half] = ("DV", mkh)
                rel += [("DV", mkh)]
            bank_users[n] = rel
            rel_all += rel
        return rel_all

    def gen_dn_unit(kbi, cb, nch_k, nxt_gen):
        buf = kbi % 2
        hbase = buf * 8
        flush_marks = [None] * 8
        if upto == 'hsdump' and kbi == 0:
            mkt = S.dma("gpsimd", "SC", out_ext.rearrange("c p t -> p c t"),
                        vD[:].rearrange("p (c t) -> p c t", c=NCH),
                        [dq_marks[("dn", kbi)], hs_ready[0]])
            S.add("gpsimd", lambda e: e.nop(), [("SC", mkt)])
            S.emit()
            return nc, True
        acc_hq_marks = {}
        for h in range(8):
            mq, n = h // 2, h % 2
            ps = psA if h % 2 == 0 else psB
            if h < 2:
                waits = [dq_marks[("dn", kbi)], hs_ready[0]] + bank_users[h % 2]
            else:
                waits = [flush_marks[h - 2]]
            last_mm = None
            for mi in range(4):
                m_glob = mq * 4 + mi
                for kk in range(nch_k):
                    j = kk
                    hs = hbase + kk
                    if cb < 5:
                        def mmfn(e, ps=ps, n=n, j=j, m_glob=m_glob, mi=mi, hs=hs, nk=nch_k):
                            return e.matmul(
                                ps[:, mi * 512:(mi + 1) * 512],
                                ldn_view(j, m_glob),
                                chunk(vD, hs, n * 512, (n + 1) * 512),
                                start=(j == 0), stop=(j == nk - 1))
                    else:
                        def mmfn(e, ps=ps, n=n, j=j, m_glob=m_glob, mi=mi, hs=hs, nk=nch_k):
                            return e.matmul(
                                ps[:, mi * 512:(mi + 1) * 512],
                                lhstl[:, j * 2048 + m_glob * 128: j * 2048 + (m_glob + 1) * 128],
                                chunk(vD, hs, n * 512, (n + 1) * 512),
                                start=(j == 0), stop=(j == nk - 1))
                    last_mm = S.add("tensor", mmfn,
                                    waits if (mi == 0 and kk == 0) else (),
                                    "PE" if (mi == 3 and kk == nch_k - 1) else None)
            mm_mark = ("PE", last_mm)
            if nxt_gen is not None:
                for _ in range(3):
                    try:
                        next(nxt_gen)
                    except StopIteration:
                        nxt_gen = None
                        break
            mkf = None
            for mi in range(4):
                m_glob = mq * 4 + mi
                key = (m_glob, n)
                dst = accB[:, m_glob * T + n * 512: m_glob * T + (n + 1) * 512]
                pt = ps[:, mi * 512:(mi + 1) * 512]
                if key not in acc_init:
                    w_ = [mm_mark, ("SX", mk_park2), mk_ctx_all]
                    mkf = S.add("vector", lambda e, dst=dst, pt=pt: e.tensor_copy(
                        out=dst, in_=pt), w_, "DV")
                    acc_init[key] = True
                else:
                    mkf = S.add("vector", lambda e, dst=dst, pt=pt: e.tensor_tensor(
                        out=dst, in0=dst, in1=pt, op=ALU.add), [mm_mark], "DV")
            flush_marks[h] = ("DV", mkf)
            for mi in range(4):
                acc_hq_marks[(mq * 4 + mi, n)] = ("DV", mkf)
            if upto == 'hq%d' % h and kbi == 0:
                mkt = S.dma("gpsimd", "SC", out_ext[0:4, :, 0:512].rearrange("c p t -> p c t"),
                            accB[:, 0:4 * T].rearrange("p (c t) -> p c t", c=4)[:, :, 0:512],
                            [flush_marks[h]])
                S.add("gpsimd", lambda e: e.nop(), [("SC", mkt)])
                S.emit()
                return nc, True
        bank_users[0] = [flush_marks[6]]
        bank_users[1] = [flush_marks[7]]
        hblk_marks[buf] = flush_marks[7]
        lhsdn_free[0] = ("PE", last_mm)
        acc_done[0] = acc_hq_marks
        return flush_marks[7], nxt_gen

    ao_marks = [None] * NCH
    wo_half_marks = [None, None]

    def evac_o(g, m, ps, tile, n, mm_mark):
        mk = S.add("scalar", lambda e, m=m, tile=tile, n=n: e.activation(
            out=chunk(xnB, m, n * 512, (n + 1) * 512), in_=tile,
            func=AF.Identity, bias=biasS[:, 48 + m:49 + m]), [mm_mark], "AC")
        if n == 1:
            ao_marks[m] = ("AC", mk)
        return [("AC", mk)]

    g = 0
    wo_gens = []

    def wo_step(k):
        while k > 0 and wo_gens:
            try:
                next(wo_gens[0])
                k -= 1
            except StopIteration:
                wo_gens.pop(0)

    pk_marks[0] = S.dma("sync", "SK0", pkgW[:], pkgu[0], [("SX", mk_park1)])
    sx_marks[0] = S.dma("sync", "SG0", sxgW[:], sgu[0], [("SX", mk_park1)])
    wo_gens.append(dqW_unit_gen(0, ("PE", lambda: wo_half_marks[0][1])))
    for mh in range(2):
        wb = wbuf0 if mh == 0 else wbuf1
        if mh == 1:
            issue_gu_dma(1, [wo_half_marks[0]])
            wo_gens.append(dqW_unit_gen(1, ("PE", lambda: wo_half_marks[1][1])))
        for mg in range(2):
            for n in range(2):
                bank = g % 2
                ps = psA if bank == 0 else psB
                waits = [(("SW%d" % mh), wo_marks[mh]), mk_ctx_all, mk_xn1_all] + bank_users[bank]
                for mi in range(4):
                    for k in range(NCH):
                        last_mm = S.add("tensor", lambda e, wb=wb, k=k, mg=mg, mi=mi, n=n, ps=ps: e.matmul(
                            ps[:, mi * 512:(mi + 1) * 512],
                            wb[:, k * 1024 + (mg * 4 + mi) * 128: k * 1024 + (mg * 4 + mi + 1) * 128],
                            chunk(vD, k, n * 512, (n + 1) * 512),
                            start=(k == 0), stop=(k == NCH - 1)),
                            waits if (mi == 0 and k == 0) else (),
                            "PE" if (mi == 3 and k == NCH - 1) else None)
                mm_mark = ("PE", last_mm)
                rel = []
                for mi in range(4):
                    m_glob = mh * 8 + mg * 4 + mi
                    rel += evac_o(g, m_glob, ps, ps[:, mi * 512:(mi + 1) * 512], n, mm_mark)
                bank_users[bank] = rel
                g += 1
                if g > 2:
                    wo_step(4)
        wo_half_marks[mh] = ("PE", last_mm)
    wo_step(99)
    mk_ao = ("AC", S.cnt["AC"])
    pre_gates = {"wo_h0": wo_half_marks[0], "wo_all": wo_half_marks[1]}

    if upto == 'ao':
        return _finish(xnB, [mk_ao])
    units = []
    for kbi, (cb, nch_dn) in enumerate(KB_BLOCKS):
        nsl = 4 if cb < 5 else 2
        for si in range(nsl):
            s = (4 * kbi + si) if cb < 5 else (20 + si)
            units.append(("gu", s, kbi, si))
        units.append(("dn", kbi, cb, nch_dn))

    mkM2, mkR2 = ln_stats(lambda c: chunk(xnB, c), ln2_sq, ln2_st, ln2_M, ln2_R,
                          ao_marks)
    mk_xp = []
    for c in range(NCH):
        S.add("vector", lambda e, c=c: e.tensor_tensor(
            out=fscr[:].bitcast(BF16)[:, 0:T], in0=chunk(xnB, c), in1=ln2_M[:], op=ALU.subtract),
            [("DV", mkR2), mk_ctx_all])
        S.add("vector", lambda e, c=c: e.tensor_tensor(
            out=chunk(xnB, c), in0=fscr[:].bitcast(BF16)[:, 0:T], in1=ln2_R[:], op=ALU.mult))
        mk = S.add("vector", lambda e, c=c: e.tensor_tensor(
            out=chunk(vD, c), in0=chunk(xE, c), in1=chunk(xnB, c), op=ALU.add),
            [("SX", mk_rel1)], "DV")
        mk_xp.append(("DV", mk))
    mk_xp_all = mk_xp[-1]

    if upto == 'xp':
        return _finish(vD, [mk_xp_all])

    mkM3, mkR3 = ln_stats(lambda c: chunk(vD, c), ln3_sq, ln3_st, ln3_M, ln3_R,
                          mk_xp)
    mk_xn3 = []
    for c in range(NCH):
        S.add("vector", lambda e, c=c: e.tensor_tensor(
            out=fscr[:].bitcast(BF16)[:, 0:T], in0=chunk(vD, c), in1=ln3_M[:], op=ALU.subtract),
            [("DV", mkR3)])
        mk = S.add("vector", lambda e, c=c: e.tensor_tensor(
            out=chunk(xnB, c), in0=fscr[:].bitcast(BF16)[:, 0:T], in1=ln3_R[:], op=ALU.mult), inc="DV")
        mk_xn3.append(("DV", mk))
    mk_xn3_all = mk_xn3[-1]

    if upto == 'xn3':
        return _finish(xnB, [mk_xn3_all])
    mk_park2 = S.dma("sync", "SX", xp2.rearrange("c p t -> p c t"),
                     vD[:].rearrange("p (c t) -> p c t", c=NCH), [mk_xp_all, mk_xn3_all])

    mk_down = None
    pend_gen = None
    carry_gen = None
    acc_done = [None]
    for i, u in enumerate(units):
        nxt = units[i + 1] if i + 1 < len(units) else None
        nxt2 = units[i + 2] if i + 2 < len(units) else None
        if u[0] == "gu":
            _, s, kbi, si = u
            if pend_gen is not None:
                drain(pend_gen)
                pend_gen = None
            gen_mm_gu(s)
            if s + 2 < NSLICE:
                issue_gu_dma(s + 2, [dq_marks[("gu", s)]])
            if nxt is not None:
                if nxt[0] == "gu" and nxt[1] < 2:
                    pass
                elif nxt[0] == "dn":
                    if carry_gen is None:
                        carry_gen = dq_unit_gen(nxt[:3])
                    drain(carry_gen)
                    carry_gen = None
                else:
                    drain(dq_unit_gen(nxt))
            hs_last = gen_ev_gu(s, kbi, si)
            if nxt2 is not None and nxt2[0] == "dn":
                carry_gen = dq_unit_gen(nxt2[:3])
                for _ in range(7):
                    try:
                        next(carry_gen)
                    except StopIteration:
                        carry_gen = None
                        break
            if si == (3 if kbi < 5 else 1):
                hs_ready[0] = hs_last[-1]
            if upto == 'hblk%d' % kbi and si == (3 if kbi < 5 else 1):
                return _finish(vD, [hs_last[-1]])
        else:
            _, kbi, cb, nch_dn = u
            g = dq_unit_gen(nxt) if nxt is not None else None
            r = gen_dn_unit(kbi, cb, nch_dn, g)
            if isinstance(r[1], bool):
                return r[0]
            mk_down, g = r
            pend_gen = g
            if upto == 'acc%d' % kbi:
                return _finish(accB, [mk_down])

    if upto == 'y':
        mky = S.dma("gpsimd", "SC", out_ext.rearrange("c p t -> p c t"),
                    accB[:].rearrange("p (c t) -> p c t", c=NCH), [mk_down])
        S.add("gpsimd", lambda e: e.nop(), [("SC", mky)])
        S.emit()
        return nc
    mk_rel2 = S.dma("sync", "SX", vD[:].rearrange("p (c t) -> p c t", c=NCH),
                    xp2.rearrange("c p t -> p c t"), [mk_down])
    acc_marks = [acc_done[0][(c, 1)] for c in range(NCH)]
    mkM4, mkR4 = ln_stats(lambda c: accB[:, c * T:(c + 1) * T], ln4_sq, ln4_st,
                          ln4_M, ln4_R, acc_marks, kind="rms_ln")
    out_half_marks = []
    for c in range(NCH):
        w_ln4 = [("DV", mkR4), ("SX", mk_rel2)]
        if c == 8:
            w_ln4.append(("SX", out_half_marks[0]))
        S.add("vector", lambda e, c=c: e.tensor_tensor(
            out=fscr[:].bitcast(BF16)[:, 0:T], in0=accB[:, c * T:(c + 1) * T],
            in1=ln4_M[:], op=ALU.subtract), w_ln4)
        S.add("vector", lambda e, c=c: e.tensor_tensor(
            out=fsb[:], in0=fscr[:].bitcast(BF16)[:, 0:T], in1=ln4_R[:], op=ALU.mult))
        mk = S.add("vector", lambda e, c=c: e.tensor_tensor(
            out=outB[:, (c % 8) * T:((c % 8) + 1) * T], in0=chunk(vD, c), in1=fsb[:],
            op=ALU.add), inc="DV")
        if c == 7:
            mh0 = S.dma("sync", "SX", out_ext[0:8].rearrange("c p t -> p c t"),
                        outB[:].rearrange("p (c t) -> p c t", c=8), [("DV", mk)])
            out_half_marks.append(mh0)
        elif c == 15:
            mh1 = S.dma("sync", "SX", out_ext[8:16].rearrange("c p t -> p c t"),
                        outB[:].rearrange("p (c t) -> p c t", c=8),
                        [("DV", mk), ("SX", out_half_marks[0])])
            out_half_marks.append(mh1)
        elif c == 8:
            pass
    S.add("sync", lambda e: e.nop(), [("SX", out_half_marks[-1])])

    S.emit()
    return nc


_NC_CACHE = {}


def kernel(L0_attn_x, L0_attn_Wq, L0_attn_bq, L0_attn_Wk, L0_attn_bk,
           L0_attn_Wv, L0_attn_bv, L0_attn_Wo, L0_attn_bo,
           L0_ffn_gate_up_packed, L0_ffn_gate_up_scales,
           L0_ffn_down_packed, L0_ffn_down_scales):
    perm = _perm_stream()
    iperm = _perm_ffn()

    x = np.asarray(L0_attn_x, np.float32)[0]
    xs = np.ascontiguousarray(x[:, perm].T)
    xs = xs.reshape(NCH, 128, NCORES, T)

    def stage_w(W, perm_rows, perm_cols):
        W = np.asarray(W, np.float32)
        if perm_rows is not None:
            W = W[perm_rows, :]
        WT = np.ascontiguousarray(W.T)
        if perm_cols is not None:
            WT = WT[perm_cols, :]
        return np.ascontiguousarray(WT.reshape(NCH, 128, D))

    wq = stage_w(L0_attn_Wq, None, perm)
    wk = stage_w(L0_attn_Wk, None, perm)
    wv = stage_w(L0_attn_Wv, None, perm)
    wo = stage_w(L0_attn_Wo, perm, None)

    bias_pack = np.zeros((128, 64), np.float32)
    bias_pack[:, 0:16] = np.asarray(L0_attn_bq, np.float32).reshape(16, 128).T
    bias_pack[:, 16:32] = np.asarray(L0_attn_bk, np.float32).reshape(16, 128).T
    bias_pack[:, 32:48] = np.asarray(L0_attn_bv, np.float32).reshape(16, 128).T
    bias_pack[:, 48:64] = np.asarray(L0_attn_bo, np.float32)[perm].reshape(16, 128).T

    gu = np.asarray(L0_ffn_gate_up_packed, np.int32)
    col_order = np.empty(GU_COLS, np.int64)
    for b in range(GU_COLS // 512):
        blk = iperm[256 * b:256 * (b + 1)]
        col_order[512 * b:512 * b + 256] = blk
        col_order[512 * b + 256:512 * (b + 1)] = blk + I_FFN
    pkgu_a = np.ascontiguousarray(gu[col_order, :].T).reshape(2, 128, GU_COLS)
    pkgu = np.empty((NSLICE, 128, 1024), np.int32)
    for s in range(NSLICE):
        pkgu[s, :, 0:512] = pkgu_a[0, :, 512 * s:512 * (s + 1)]
        pkgu[s, :, 512:1024] = pkgu_a[1, :, 512 * s:512 * (s + 1)]

    gsc = np.asarray(L0_ffn_gate_up_scales)
    gsc = gsc[:, col_order]
    sgu_a = np.empty((2, 128, 2 * GU_COLS), np.float16)
    for A in range(2):
        sgu_a[A] = np.repeat(np.repeat(gsc[8 * A:8 * (A + 1)], 16, axis=0), 2, axis=1)
    sgu = np.empty((NSLICE, 128, 2048), np.float16)
    for s in range(NSLICE):
        sgu[s, :, 0:1024] = sgu_a[0, :, 1024 * s:1024 * (s + 1)]
        sgu[s, :, 1024:2048] = sgu_a[1, :, 1024 * s:1024 * (s + 1)]

    dp = np.asarray(L0_ffn_down_packed, np.int32)
    dpT = np.ascontiguousarray(dp[perm, :].T)
    pkdn = np.empty((768, D), np.int32)
    pkdn[0:704] = dpT
    pkdn[704:768] = dpT[640:704]
    dsc = np.asarray(L0_ffn_down_scales)
    dsc = dsc[:, perm]
    sdn_full = np.repeat(np.repeat(dsc, 16, axis=0), 2, axis=1)
    sdn = np.empty((768, 2 * D), np.float16)
    sdn[0:704] = sdn_full
    sdn[704:768] = sdn_full[640:704]

    ident = np.eye(128, dtype=np.float32)

    key = "v1"
    if key not in _NC_CACHE:
        _NC_CACHE[key] = build_program()
    nc = _NC_CACHE[key]

    in_maps = []
    for i in range(NCORES):
        cm = np.zeros((8, 1), np.float32)
        cm[:i, 0] = 1.0
        in_maps.append({
            "x_in": np.ascontiguousarray(xs[:, :, i, :]),
            "w_q": wq, "w_k": wk, "w_v": wv, "w_o": wo,
            "bias_pack": bias_pack, "pkgu": pkgu, "sgu": sgu,
            "pkdn": pkdn, "sdn": sdn, "cmask_in": cm, "idin": ident,
        })
    res = run_bass_kernel_spmd(nc, in_maps, core_ids=list(range(NCORES)))

    out = np.empty((8192, D), np.float32)
    for i in range(NCORES):
        o = np.asarray(res.results[i]["out_ext"], np.float32)
        out[T * i:T * (i + 1), :] = o.reshape(D, T).T
    inv = np.empty_like(out)
    inv[:, perm] = out
    return inv.reshape(1, 8192, D)
